# version 06
# baseline (speedup 1.0000x reference)
import contextlib
import numpy as np
import ml_dtypes
import concourse.bass as bass
import concourse.mybir as mybir
from concourse.bass_utils import run_bass_kernel_spmd

F32 = mybir.dt.float32
BF16 = mybir.dt.bfloat16
AF = mybir.ActivationFunctionType
ALU = mybir.AluOpType
AXL = mybir.AxisListType

PI = float(np.pi)
MAGIC = 12582912.0
CLAMP_MAX = float(np.log(1.0 / 0.01))
BF = ml_dtypes.bfloat16

B, T, DM = 4, 4096, 1024
H = 8
TOWN = 2048
NCORES = 8
NCH = 32
NG = 8

ENGS = ("sync", "tensor", "vector", "scalar", "gpsimd")
SY, TE, V, S, G_ = ENGS


class Buf:
    __slots__ = ("name", "writer", "readers", "exclusive")

    def __init__(self, name, seed=(), exclusive=False):
        self.name = name
        self.writer = None
        self.readers = list(seed)
        self.exclusive = exclusive


class Em:

    def __init__(self):
        self.recs = {e: [] for e in ENGS}
        self.counts = {}
        self.waited = {e: {} for e in ENGS}
        self.free_pool = []
        self.forced = set()
        self.dma_sems = set()

    def op(self, eng, fn, reads=(), writes=(), deps=(), inc=1, force=False,
           sem=None):
        sem = sem or eng
        if sem != eng:
            self.dma_sems.add(sem)
        toks = list(deps)
        for b in reads:
            if b.writer is not None:
                toks.append(b.writer)
            if b.exclusive:
                toks.extend(b.readers)
        for b in writes:
            if b.writer is not None:
                toks.append(b.writer)
            toks.extend(b.readers)
        need = {}
        for (sm, vv) in toks:
            if sm == eng == TE:
                continue
            if vv > need.get(sm, -1):
                need[sm] = vv
        waits = []
        for sm in sorted(need):
            vv = need[sm]
            if vv > self.waited[eng].get(sm, 0):
                waits.append((sm, vv))
                self.waited[eng][sm] = vv
        self.counts[sem] = self.counts.get(sem, 0) + inc
        tok = (sem, self.counts[sem])
        if force:
            self.forced.add(tok)
        self.recs[eng].append(("call", fn, waits, inc, self.counts[sem], sem))
        for b in reads:
            b.readers.append(tok)
        for b in writes:
            b.writer = tok
            b.readers = []
        return tok

    def barrier(self):
        for sm in sorted(self.dma_sems):
            v = self.counts.get(sm, 0)
            if v > self.waited[SY].get(sm, 0):
                self.recs[SY].append(("wait", sm, v))
                self.waited[SY][sm] = v
        vals = {}
        for e in ENGS:
            v = self.counts.get(e, 0)
            vals[e] = v
            if v:
                self.forced.add((e, v))
        for e in ENGS:
            for o in ENGS:
                if o != e and vals[o] > self.waited[e].get(o, 0):
                    self.recs[e].append(("wait", o, vals[o]))
                    self.waited[e][o] = vals[o]
        self.free_pool = []

    def final_wait(self, eng, sems_to_wait):
        for sm in sems_to_wait:
            self.recs[eng].append(("wait", sm, self.counts.get(sm, 0)))

    DENSE = False

    def finalize(self):
        needed = {}
        if Em.DENSE:
            for e in ENGS:
                for rec in self.recs[e]:
                    if rec[0] == "call":
                        needed.setdefault(rec[5], set()).add(rec[4])
                    elif rec[0] == "inc":
                        needed.setdefault(e, set()).add(rec[1])
        for (sm, v) in self.forced:
            needed.setdefault(sm, set()).add(v)
        for e in ENGS:
            for rec in self.recs[e]:
                if rec[0] == "wait":
                    needed.setdefault(rec[1], set()).add(rec[2])
                elif rec[0] == "call":
                    for (sm, vv) in rec[2]:
                        needed.setdefault(sm, set()).add(vv)
        remap = {}
        rv = {}
        for e in ENGS:
            new = []
            for rec in self.recs[e]:
                if rec[0] == "inc":
                    p = rec[1]
                    if p in needed.get(e, ()):
                        r = rv.get(e, 0) + 1
                        rv[e] = r
                        remap.setdefault(e, {})[p] = r
                        new.append(("inc",))
                elif rec[0] == "call":
                    _, fn, waits, inc, pval, sem = rec
                    if pval in needed.get(sem, ()):
                        r = rv.get(sem, 0) + inc
                        rv[sem] = r
                        remap.setdefault(sem, {})[pval] = r
                        new.append(("call", fn, waits, inc, sem))
                    else:
                        new.append(("call", fn, waits, 0, sem))
                else:
                    new.append(rec)
            self.recs[e] = new
        for e in ENGS:
            out = []
            for rec in self.recs[e]:
                if rec[0] == "wait":
                    out.append(("wait", rec[1], remap[rec[1]][rec[2]]))
                elif rec[0] == "call":
                    _, fn, waits, inc, sem = rec
                    out.append(("call", fn,
                                [(sm, remap[sm][vv]) for (sm, vv) in waits],
                                inc, sem))
                else:
                    out.append(rec)
            self.recs[e] = out

    TE_FENCE = None

    def replay(self, eng_name, eng_obj, sems):
        own = sems[eng_name]
        for rec in self.recs[eng_name]:
            k = rec[0]
            if k == "wait":
                eng_obj.wait_ge(sems[rec[1]], rec[2])
            elif k == "inc":
                eng_obj.drain(fusable=False).then_inc(own, 1)
            else:
                _, fn, waits, inc, sem = rec
                for (sm, vv) in waits:
                    eng_obj.wait_ge(sems[sm], vv)
                inst = fn(eng_obj)
                if inc:
                    if eng_name != "sync" and Em.TE_FENCE:
                        eng_obj.drain(fusable=False).then_inc(sems[sem], inc)
                    else:
                        inst.then_inc(sems[sem], inc)


class View:
    __slots__ = ("ar", "woff", "P", "N", "dt")

    def __init__(self, ar, woff, P, N, dt):
        self.ar, self.woff, self.P, self.N, self.dt = ar, woff, P, N, dt

    def __getitem__(self, idx):
        ps, cs = idx
        c0 = cs.start if cs.start is not None else 0
        c1 = cs.stop if cs.stop is not None else self.N
        if self.dt == F32:
            return self.ar.tf[ps, self.woff + c0: self.woff + c1]
        return self.ar.tb[ps, 2 * self.woff + c0: 2 * self.woff + c1]


class Arena:
    def __init__(self, nc, em, words):
        self.tf = nc.alloc_sbuf_tensor("arena", [128, words], F32)
        self.tb = self.tf.bitcast(BF16)
        self.words = words
        self.em = em
        self.lo = 0
        self.hi = words
        self.stacks = {"L": [], "R": []}

    def push(self, side):
        self.stacks[side].append((self.lo if side == "L" else self.hi, []))

    def pop(self, side):
        mark, bufs = self.stacks[side].pop()
        for b in bufs:
            if b.writer is not None:
                self.em.free_pool.append(b.writer)
            self.em.free_pool.extend(b.readers)
        if side == "L":
            self.lo = mark
        else:
            self.hi = mark

    def alloc(self, name, P, N, dt, side="L"):
        words = N if dt == F32 else (N + 1) // 2
        words = (words + 15) & ~15
        if side == "L":
            off = self.lo
            self.lo += words
        else:
            self.hi -= words
            off = self.hi
        assert self.lo <= self.hi, "arena overflow at %s (%d > %d)" % (
            name, self.lo, self.hi)
        buf = Buf(name, seed=self.em.free_pool)
        if self.stacks[side]:
            self.stacks[side][-1][1].append(buf)
        return View(self, off, P, N, dt), buf


def build_program(limit=None):
    nc = bass.Bass("TRN2", target_bir_lowering=False, debug=False)
    em = Em()

    def din(name, shape, dt=F32):
        return nc.dram_tensor(name, list(shape), dt, kind="ExternalInput").ap()

    xbf_h = din("xbf", [8, 128, T], BF16)
    xf32_h = din("xf32", [8, 128, TOWN], F32)
    wh_h = din("w_qkvu", [H, 128, 4096], BF16)
    wo_h = din("w_out", [128, 8192], BF16)
    wf_h = din("w_ffn", [128, 8192], BF16)
    wp_h = din("w_proj", [128, 2048], BF16)
    wd_h = din("w_down", [128, 120], BF16)
    wu_h = din("w_up", [15, 1024], BF16)
    th_h = din("theta_lay", [128, H], F32)
    qb_h = din("qkvu_b_lay", [128, 32], F32)
    bv_h = din("v_b_row", [1, 1024], BF16)
    ob_h = din("out_b_lay", [128, 8], F32)
    bp_h = din("bp_lay", [128, 2], F32)
    sim_h = din("sim_t", [4, 256], F32)
    slope_h = din("slope_row", [1, 8], F32)
    temp_h = din("temp_row", [1, 1], F32)
    aux_h = din("aux_row", [1, 4], F32)

    out_h = nc.dram_tensor("out", [8, 128, TOWN], F32, kind="ExternalOutput").ap()
    dbg_h = nc.dram_tensor("dbg", [128, 8 * T], BF16, kind="ExternalOutput").ap() \
        if (limit is not None) else None
    stop_dump = []
    scratch_h = nc.dram_tensor("scratch", [1, 64], F32, kind="Internal").ap()
    b_scr = Buf("scratch")

    oneskd_i = nc.inline_tensor(np.full((128, 128), 1.0 / DM, dtype=BF), "oneskd_i")
    ones_i = nc.inline_tensor(np.ones((128, 128), dtype=BF), "ones_i")
    identb_i = nc.inline_tensor(np.eye(128, dtype=np.float32).astype(BF), "identb_i")
    identf_i = nc.inline_tensor(np.eye(128, dtype=np.float32), "identf_i")
    onesrow_i = nc.inline_tensor(np.ones((1, 128), dtype=BF), "onesrow_i")

    stat = {}

    def salloc(name, shape, dt):
        t = nc.alloc_sbuf_tensor(name, list(shape), dt)
        b = Buf(name)
        stat[name] = (t, b)
        return t, b

    oneskd, b_oneskd = salloc("oneskd", [128, 128], BF16)
    onesb, b_onesb = salloc("onesb", [128, 128], BF16)
    identb, b_identb = salloc("identb", [128, 128], BF16)
    identf, b_identf = salloc("identf", [128, 128], F32)
    onesrow, b_onesrow = salloc("onesrow", [1, 128], BF16)
    thlay, b_thlay = salloc("thlay", [128, H], F32)
    th2pi, b_th2pi = salloc("th2pi", [128, H], F32)
    thoffq, b_thoffq = salloc("thoffq", [128, 32], F32)
    koffq, b_koffq = salloc("koffq", [128, 32], F32)
    offqt, b_offqt = salloc("offqt", [128, 4], F32)
    qbl, b_qbl = salloc("qbl", [128, 32], F32)
    bvrow, b_bvrow = salloc("bvrow", [1, 1024], BF16)
    obl, b_obl = salloc("obl", [128, 8], F32)
    bpl, b_bpl = salloc("bpl", [128, 2], F32)
    simt, b_simt = salloc("simt", [4, 256], F32)
    simtb, b_simtb = salloc("simtb", [4, 256], BF16)
    simsq, b_simsq = salloc("simsq", [4, 256], F32)
    simrn, b_simrn = salloc("simrn", [4, 1], F32)
    simn, b_simn = salloc("simn", [128, 8], BF16)
    slopes, b_slopes = salloc("slopes", [1, 8], F32)
    temps, b_temps = salloc("temps", [1, 1], F32)
    auxs, b_auxs = salloc("auxs", [1, 4], F32)
    srow, b_srow = salloc("srow", [1, 24], F32)
    smalls, b_smalls = salloc("smalls", [128, 24], F32)
    maskh, b_maskh = salloc("maskh", [128, 128], F32)
    epsln, b_epsln = salloc("epsln", [128, 1], F32)
    epsrms, b_epsrms = salloc("epsrms", [128, 1], F32)
    epsl2, b_epsl2 = salloc("epsl2", [128, 1], F32)
    pihalf, b_pihalf = salloc("pihalf", [128, 1], F32)
    ibase, b_ibase = salloc("ibase", [128, 1024], F32)
    wu, b_wu = salloc("wu", [15, 1024], BF16)
    wd, b_wd = salloc("wd", [128, 120], BF16)
    hpb, b_hpb = salloc("hpb", [15, TOWN], BF16)
    ret15, b_ret15 = salloc("ret15", [15, TOWN], BF16)
    mxt, b_mxt = salloc("mxt", [128, 1], F32)
    dvt, b_dvt = salloc("dvt", [128, 4], F32)
    sgt, b_sgt = salloc("sgt", [128, 4], F32)
    bet, b_bet = salloc("bet", [128, 1], F32)
    ext, b_ext = salloc("ext", [128, 4], F32)
    sut, b_sut = salloc("sut", [128, 1], F32)
    lwt, b_lwt = salloc("lwt", [128, 4], F32)
    rtt, b_rtt = salloc("rtt", [128, 15], F32)

    psum = []
    for i in range(8):
        t = nc.alloc_psum_tensor("pbank%d" % i, [128, 512], F32)
        psum.append((t, Buf("pbank%d" % i, exclusive=True)))
    psum_bf = {i: (psum[i][0].bitcast(BF16), psum[i][1]) for i in (6, 7)}

    arena = Arena(nc, em, (nc.sbuf_bytes_remaining // 4) - 32)

    def dma(dst_ap, src_ap, reads, writes, force=False):
        wl, rl = list(writes), list(reads)
        sem = "d_" + (wl[0].name if wl else rl[0].name)
        return em.op(SY, lambda e: e.dma_start(out=dst_ap, in_=src_ap),
                     reads=rl, writes=wl, inc=16, force=force, sem=sem)

    dma(oneskd[:], oneskd_i.ap()[:, :], [], [b_oneskd])
    dma(onesb[:], ones_i.ap()[:, :], [], [b_onesb])
    dma(identb[:], identb_i.ap()[:, :], [], [b_identb])
    dma(identf[:], identf_i.ap()[:, :], [], [b_identf])
    dma(onesrow[:], onesrow_i.ap()[:, :], [], [b_onesrow])
    dma(thlay[:], th_h[:, :], [], [b_thlay])
    dma(qbl[:], qb_h[:, :], [], [b_qbl])
    dma(bvrow[:], bv_h[:, :], [], [b_bvrow])
    dma(obl[:], ob_h[:, :], [], [b_obl])
    dma(bpl[:], bp_h[:, :], [], [b_bpl])
    dma(simt[:], sim_h[:, :], [], [b_simt])
    dma(slopes[:], slope_h[:, :], [], [b_slopes])
    dma(temps[:], temp_h[:, :], [], [b_temps])
    dma(auxs[:], aux_h[:, :], [], [b_auxs])

    arena.push("L")
    xbf, b_xbf = arena.alloc("xbf", 128, 8 * T, BF16)
    bxb = [[Buf("xbf_c%d_g%d" % (c, g)) for g in range(8)] for c in range(8)]
    wh0, b_wh0 = arena.alloc("wh0", 128, 4096, BF16)
    klo, b_klo = arena.alloc("klo", 128, T, BF16)
    khi, b_khi = arena.alloc("khi", 128, T, BF16)
    qlo, b_qlo = arena.alloc("qlo", 128, TOWN, BF16)
    qhi, b_qhi = arena.alloc("qhi", 128, TOWN, BF16)

    arena.push("R")
    gated, b_gated = arena.alloc("gated", 128, 16 if limit in ("noalias", "snap", "mega") else 8 * TOWN,
                                 BF16, "R")
    rmsac, b_rmsac = arena.alloc("rmsac", 128, 16 if limit in ("noalias", "snap", "mega") else TOWN,
                                 F32, "R")

    for c in range(8):
        dma(xbf[:, c * T:(c + 1) * T], xbf_h[c, :, :], [], bxb[c])
    dma(wh0[:, :], wh_h[0, :, :], [], [b_wh0])

    em.op(V, lambda e: e.memset(epsln[:], 1e-5), writes=[b_epsln])
    em.op(V, lambda e: e.memset(epsrms[:], 1e-6), writes=[b_epsrms])
    em.op(V, lambda e: e.memset(epsl2[:], 1e-24), writes=[b_epsl2])
    em.op(V, lambda e: e.memset(pihalf[:], PI / 2), writes=[b_pihalf])
    em.op(V, lambda e: e.memset(rmsac[:, :], 0.0), writes=[b_rmsac])
    em.op(G_, lambda e: e.iota(ibase[:], pattern=[[1, 1024]], base=0,
                               channel_multiplier=0,
                               allow_small_or_imprecise_dtypes=True),
          writes=[b_ibase])

    def emit_smalls_chain():
        em.op(V, lambda e: e.memset(srow[:, :], 0.0), writes=[b_srow])
        em.op(S, lambda e: e.activation(out=srow[0:1, 0:8], in_=slopes[:, :], func=AF.Exp),
              reads=[b_srow, b_slopes], writes=[b_srow])
        em.op(V, lambda e: e.tensor_scalar(out=srow[0:1, 0:8], in0=srow[0:1, 0:8],
                                           scalar1=1.0, scalar2=None, op0=ALU.subtract),
              reads=[b_srow], writes=[b_srow])
        em.op(V, lambda e: e.tensor_scalar(out=srow[0:1, 8:16], in0=srow[0:1, 0:8],
                                           scalar1=auxs[0:1, 0:1], scalar2=1.0,
                                           op0=ALU.mult, op1=ALU.add),
              reads=[b_srow, b_auxs], writes=[b_srow])
        em.op(V, lambda e: e.tensor_scalar(out=srow[0:1, 18:19], in0=temps[:, :],
                                           scalar1=CLAMP_MAX, scalar2=None, op0=ALU.min),
              reads=[b_temps], writes=[b_srow])
        em.op(S, lambda e: e.activation(out=srow[0:1, 17:18], in_=srow[0:1, 18:19],
                                        func=AF.Exp),
              reads=[b_srow], writes=[b_srow])
        em.op(V, lambda e: e.tensor_scalar(out=srow[0:1, 16:17], in0=srow[0:1, 17:18],
                                           scalar1=-1.0, scalar2=None, op0=ALU.mult),
              reads=[b_srow], writes=[b_srow])
        em.op(V, lambda e: e.tensor_copy(out=srow[0:1, 19:22], in_=auxs[0:1, 0:3]),
              reads=[b_auxs], writes=[b_srow])
        dma(scratch_h[0:1, 0:24], srow[:, :], [b_srow], [b_scr])
        dma(smalls[:], scratch_h[0:1, 0:24].to_broadcast((128, 24)), [b_scr], [b_smalls])

        em.op(V, lambda e: e.tensor_scalar(out=th2pi[:], in0=thlay[:, :],
                                           scalar1=1.0 / (2 * PI), scalar2=None,
                                           op0=ALU.mult),
              reads=[b_thlay], writes=[b_th2pi])
        for q in range(4):
            scol = 20 if q < 2 else 21

            def mkoff(e, q=q, scol=scol):
                return e.tensor_scalar(out=offqt[:, q:q + 1], in0=smalls[:, scol:scol + 1],
                                       scalar1=float(1024 * q), scalar2=None, op0=ALU.add)
            em.op(V, mkoff, reads=[b_smalls], writes=[b_offqt])

            def mkth(e, q=q):
                return e.tensor_scalar(out=thoffq[:, q * 8:(q + 1) * 8], in0=thlay[:, :],
                                       scalar1=offqt[:, q:q + 1], scalar2=None,
                                       op0=ALU.mult)
            em.op(V, mkth, reads=[b_thlay, b_offqt], writes=[b_thoffq])

            def mkko(e, q=q):
                return e.tensor_scalar(out=koffq[:, q * 8:(q + 1) * 8], in0=th2pi[:, :],
                                       scalar1=offqt[:, q:q + 1], scalar2=None,
                                       op0=ALU.mult)
            em.op(V, mkko, reads=[b_th2pi, b_offqt], writes=[b_koffq])

        em.op(V, lambda e: e.tensor_tensor(out=simsq[:], in0=simt[:], in1=simt[:],
                                           op=ALU.mult),
              reads=[b_simt], writes=[b_simsq])
        em.op(V, lambda e: e.reduce_sum(out=simrn[:], in_=simsq[:], axis=AXL.X),
              reads=[b_simsq], writes=[b_simrn])
        em.op(S, lambda e: e.activation(out=simrn[:], in_=simrn[:], func=AF.Ln,
                                        bias=epsl2[0:4, 0:1]),
              reads=[b_epsl2, b_simrn], writes=[b_simrn])
        em.op(S, lambda e: e.activation(out=simrn[:], in_=simrn[:], func=AF.Exp,
                                        scale=-0.5),
              reads=[b_simrn], writes=[b_simrn])
        em.op(V, lambda e: e.tensor_scalar(out=simtb[:], in0=simt[:],
                                           scalar1=simrn[0:4, 0:1], scalar2=None,
                                           op0=ALU.mult),
              reads=[b_simt, b_simrn], writes=[b_simtb])

    def emit_simn_tr():
        for pc in range(2):
            def trsim(e, pc=pc):
                return e.transpose(psum_bf[6][0][0:128, 0:4],
                                   simtb[:, pc * 128:(pc + 1) * 128], identb[0:4, 0:4])
            em.op(TE, trsim, reads=[b_simtb, b_identb], writes=[psum_bf[6][1]])

            def evsim(e, pc=pc):
                return e.tensor_copy(out=simn[:, pc * 4:(pc + 1) * 4],
                                     in_=psum_bf[6][0][0:128, 0:4])
            em.op(V, evsim, reads=[psum_bf[6][1]], writes=[b_simn])


    arena.push("R")
    varsb, b_varsb = arena.alloc("varsb", 128, 512, F32, "R")
    var2sb, b_var2sb = arena.alloc("var2sb", 128, 512, F32, "R")
    rsb, b_rsb = arena.alloc("rsb", 128, 512, F32, "R")
    t1sb, b_t1sb = arena.alloc("t1sb", 128, 512, F32, "R")
    xsqs = [arena.alloc("xsq%d" % i, 128, 512, BF16, "R") for i in range(2)]

    for grp in range(8):
        g0 = grp * 512
        pm, b_pm = psum[grp % 2]
        ps2, b_ps2 = psum[2 + grp % 2]
        for c in range(8):
            def mm_m(e, c=c, g0=g0, pm=pm):
                return e.matmul(pm[:, 0:512], oneskd[:],
                                xbf[:, c * T + g0: c * T + g0 + 512],
                                start=(c == 0), stop=(c == 7))
            em.op(TE, mm_m, reads=[b_oneskd, bxb[c][grp]], writes=[b_pm])
        for c in range(8):
            xsq, b_xsq = xsqs[c % 2]

            def sq(e, c=c, g0=g0, xsq=xsq):
                return e.activation(out=xsq[:, 0:512], func=AF.Square,
                                    in_=xbf[:, c * T + g0: c * T + g0 + 512])
            em.op(S, sq, reads=[bxb[c][grp]], writes=[b_xsq])

            def mm_s2(e, c=c, xsq=xsq, ps2=ps2):
                return e.matmul(ps2[:, 0:512], oneskd[:], xsq[:, 0:512],
                                start=(c == 0), stop=(c == 7))
            em.op(TE, mm_s2, reads=[b_oneskd, b_xsq], writes=[b_ps2])
        em.op(S, lambda e, pm=pm, varsb=varsb: e.activation(
                  out=varsb[:, 0:512], in_=pm[:, 0:512], func=AF.Square),
              reads=[b_pm], writes=[b_varsb])
        em.op(V, lambda e, ps2=ps2, varsb=varsb, var2sb=var2sb: e.tensor_tensor(
                  out=var2sb[:, 0:512], in0=ps2[:, 0:512], in1=varsb[:, 0:512],
                  op=ALU.subtract),
              reads=[b_ps2, b_varsb], writes=[b_var2sb])
        em.op(S, lambda e, var2sb=var2sb: e.activation(
                  out=var2sb[:, 0:512], in_=var2sb[:, 0:512],
                  func=AF.Ln, bias=epsln[:, 0:1]),
              reads=[b_epsln, b_var2sb], writes=[b_var2sb])
        em.op(S, lambda e, rsb=rsb, var2sb=var2sb: e.activation(
                  out=rsb[:, 0:512], in_=var2sb[:, 0:512],
                  func=AF.Exp, scale=-0.5),
              reads=[b_var2sb], writes=[b_rsb])
        for c in range(8):
            def sub(e, c=c, g0=g0, pm=pm, t1sb=t1sb):
                return e.tensor_tensor(out=t1sb[:, 0:512],
                                       in0=xbf[:, c * T + g0: c * T + g0 + 512],
                                       in1=pm[:, 0:512], op=ALU.subtract)
            em.op(V, sub, reads=[bxb[c][grp], b_pm], writes=[b_t1sb])

            def mul(e, c=c, g0=g0, t1sb=t1sb, rsb=rsb):
                return e.tensor_tensor(out=xbf[:, c * T + g0: c * T + g0 + 512],
                                       in0=t1sb[:, 0:512], in1=rsb[:, 0:512],
                                       op=ALU.mult)
            em.op(V, mul, reads=[b_t1sb, b_rsb], writes=[bxb[c][grp]])
        if grp == 0:
            emit_smalls_chain()
    emit_simn_tr()
    arena.pop("R")

    em.barrier()

    def finish():
        em.finalize()
        es = contextlib.ExitStack()
        sems = {e: es.enter_context(nc.semaphore("sem_" + e)) for e in ENGS}
        for sm in sorted(em.dma_sems):
            sems[sm] = es.enter_context(nc.semaphore("sem_" + sm))
        block = es.enter_context(nc.Block())

        @block.sync
        def _(eng):
            em.replay("sync", eng, sems)

        @block.tensor
        def _(eng):
            em.replay("tensor", eng, sems)

        @block.vector
        def _(eng):
            em.replay("vector", eng, sems)

        @block.scalar
        def _(eng):
            em.replay("scalar", eng, sems)

        @block.gpsimd
        def _(eng):
            em.replay("gpsimd", eng, sems)

        es.close()
        return nc

    if limit == "ln1":
        dma(dbg_h[:, :], xbf[:, 0:8 * T], [b for row in bxb for b in row], [], force=True)
        em.final_wait(SY, ["d_xbf"])
        return finish()
    if limit == "smalls":
        smf = nc.dram_tensor("smf", [128, 64], F32, kind="ExternalOutput").ap()
        b_smf = Buf("smf")
        em.barrier()
        dma(smf[:, 0:24], smalls[:, 0:24], [b_smalls], [b_smf], force=True)
        dma(smf[:, 24:56], thoffq[:, 0:32], [b_thoffq], [b_smf], force=True)
        em.final_wait(SY, ["d_smf"])
        return finish()

    QO, KO, VO, UO = 0, 128, 256, 384

    for h in range(H):
        al_ap = smalls[:, h:h + 1]
        c1_ap = smalls[:, 8 + h:9 + h]

        def mkmask(e, h=h):
            return e.affine_select(maskh[:],
                                   smalls[:, h:h + 1].broadcast_to([128, 128]),
                                   pattern=[[1, 128]], compare_op=ALU.is_ge,
                                   fill=0.0, base=0, channel_multiplier=-1)
        em.op(G_, mkmask, reads=[b_smalls], writes=[b_maskh])

        arena.push("L")
        thbufs = [arena.alloc("thbuf%d_%d" % (i, h), 128, 1024, F32)
                  for i in range(2)]
        kbufs = [arena.alloc("kbuf%d_%d" % (i, h), 128, 1024, F32)
                 for i in range(2)]
        sinb, b_sinb = arena.alloc("sinb_%d" % h, 128, T, BF16)
        cosb, b_cosb = arena.alloc("cosb_%d" % h, 128, T, BF16)
        ksil, b_ksil = arena.alloc("ksil_%d" % h, 128, T, BF16)
        qsil, b_qsil = arena.alloc("qsil_%d" % h, 128, TOWN, BF16)

        for q in range(4):
            qs = q * 1024
            thbuf, b_thbuf = thbufs[q % 2]
            kbuf, b_kbuf = kbufs[q % 2]

            def opth(e, q=q, h=h, thbuf=thbuf):
                return e.tensor_scalar(out=thbuf[:, 0:1024], in0=ibase[:],
                                       scalar1=thlay[:, h:h + 1],
                                       scalar2=thoffq[:, q * 8 + h: q * 8 + h + 1],
                                       op0=ALU.mult, op1=ALU.add)
            em.op(V, opth, reads=[b_ibase, b_thlay, b_thoffq], writes=[b_thbuf])

            def opk1(e, q=q, h=h, kbuf=kbuf):
                return e.tensor_scalar(out=kbuf[:, 0:1024], in0=ibase[:],
                                       scalar1=th2pi[:, h:h + 1],
                                       scalar2=koffq[:, q * 8 + h: q * 8 + h + 1],
                                       op0=ALU.mult, op1=ALU.add)
            em.op(V, opk1, reads=[b_ibase, b_th2pi, b_koffq], writes=[b_kbuf])
            em.op(G_, lambda e, kbuf=kbuf: e.tensor_scalar(
                      out=kbuf[:, 0:1024],
                      in0=kbuf[:, 0:1024], scalar1=MAGIC,
                      scalar2=MAGIC, op0=ALU.add,
                      op1=ALU.subtract),
                  reads=[b_kbuf], writes=[b_kbuf])
            em.op(V, lambda e, kbuf=kbuf, thbuf=thbuf: e.scalar_tensor_tensor(
                      out=kbuf[:, 0:1024],
                      in0=kbuf[:, 0:1024],
                      scalar=-2 * PI,
                      in1=thbuf[:, 0:1024],
                      op0=ALU.mult, op1=ALU.add),
                  reads=[b_kbuf, b_thbuf], writes=[b_kbuf])
            em.op(G_, lambda e, kbuf=kbuf: e.tensor_scalar(
                      out=kbuf[:, 0:1024],
                      in0=kbuf[:, 0:1024],
                      scalar1=float(np.float32(PI)),
                      scalar2=-float(np.float32(PI)),
                      op0=ALU.min, op1=ALU.max),
                  reads=[b_kbuf], writes=[b_kbuf])

            def opsin(e, qs=qs, kbuf=kbuf):
                return e.activation(out=sinb[:, qs:qs + 1024], in_=kbuf[:, 0:1024],
                                    func=AF.Sin)
            em.op(S, opsin, reads=[b_kbuf], writes=[b_sinb])
            em.op(S, lambda e, kbuf=kbuf, thbuf=thbuf: e.activation(
                      out=thbuf[:, 0:1024], in_=kbuf[:, 0:1024],
                      func=AF.Abs),
                  reads=[b_kbuf], writes=[b_thbuf])

            def opcos(e, qs=qs, thbuf=thbuf):
                return e.activation(out=cosb[:, qs:qs + 1024], in_=thbuf[:, 0:1024],
                                    func=AF.Sin, scale=-1.0, bias=pihalf[:, 0:1])
            em.op(S, opcos, reads=[b_thbuf, b_pihalf], writes=[b_cosb])

        for nt in range(8):
            pp, b_pp = psum[4 + nt % 2]
            for c in range(8):
                def mmk(e, c=c, nt=nt, pp=pp):
                    return e.matmul(pp[:, 0:512],
                                    wh0[:, c * 512 + KO: c * 512 + KO + 128],
                                    xbf[:, c * T + nt * 512: c * T + (nt + 1) * 512],
                                    start=(c == 0), stop=(c == 7))
                em.op(TE, mmk, reads=[b_wh0, bxb[c][nt]], writes=[b_pp])

            def evk(e, nt=nt, pp=pp, h=h):
                return e.activation(out=ksil[:, nt * 512:(nt + 1) * 512],
                                    in_=pp[:, 0:512], func=AF.Silu,
                                    bias=qbl[:, 8 + h: 9 + h])
            em.op(S, evk, reads=[b_pp, b_qbl], writes=[b_ksil])
        em.op(V, lambda e: e.tensor_tensor(out=klo[:, :], in0=ksil[:, :],
                                           in1=cosb[:, :], op=ALU.mult),
              reads=[b_ksil, b_cosb], writes=[b_klo])
        em.op(V, lambda e: e.tensor_tensor(out=khi[:, :], in0=ksil[:, :],
                                           in1=sinb[:, :], op=ALU.mult),
              reads=[b_ksil, b_sinb], writes=[b_khi])

        for nt in range(4):
            pp, b_pp = psum[4 + nt % 2]
            for c in range(8):
                def mmq(e, c=c, nt=nt, pp=pp):
                    return e.matmul(pp[:, 0:512],
                                    wh0[:, c * 512 + QO: c * 512 + QO + 128],
                                    xbf[:, c * T + nt * 512: c * T + (nt + 1) * 512],
                                    start=(c == 0), stop=(c == 7))
                em.op(TE, mmq, reads=[b_wh0, bxb[c][nt]], writes=[b_pp])

            def evq(e, nt=nt, pp=pp, h=h):
                return e.activation(out=qsil[:, nt * 512:(nt + 1) * 512],
                                    in_=pp[:, 0:512], func=AF.Silu,
                                    bias=qbl[:, h: h + 1])
            em.op(S, evq, reads=[b_pp, b_qbl], writes=[b_qsil])
        em.op(V, lambda e: e.tensor_tensor(out=qlo[:, :], in0=qsil[:, :],
                                           in1=cosb[:, 0:TOWN], op=ALU.mult),
              reads=[b_qsil, b_cosb], writes=[b_qlo])
        em.op(V, lambda e: e.tensor_tensor(out=qhi[:, :], in0=qsil[:, :],
                                           in1=sinb[:, 0:TOWN], op=ALU.mult),
              reads=[b_qsil, b_sinb], writes=[b_qhi])

        if limit == "h0prod" and h == 0:
            stop_dump.append((klo, b_klo, T))
            arena.pop("L")
            break

        if limit not in ("noalias", "mega"):
            arena.pop("L")

        arena.push("L")
        if limit == "noalias":
            pass
        klot, b_klot = arena.alloc("klot_%d" % h, 128, T, BF16)
        khit, b_khit = arena.alloc("khit_%d" % h, 128, T, BF16)
        vtm, b_vtm = arena.alloc("vtm_%d" % h, 128, T, BF16)
        ub, b_ub = arena.alloc("ub_%d" % h, 128, TOWN, BF16)
        Msb, b_Msb = arena.alloc("Msb_%d" % h, 128, NG * 256, F32)
        Mbf, b_Mbf = arena.alloc("Mbf_%d" % h, 128, NG * 256, BF16)
        etsb, b_etsb = arena.alloc("etsb_%d" % h, 128, 384, BF16)
        asq, b_asq = arena.alloc("asq_%d" % h, 128, 512, BF16)

        for nt in range(4):
            pp, b_pp = psum[4 + nt % 2]
            for c in range(8):
                def mmu(e, c=c, nt=nt, pp=pp):
                    return e.matmul(pp[:, 0:512],
                                    wh0[:, c * 512 + UO: c * 512 + UO + 128],
                                    xbf[:, c * T + nt * 512: c * T + (nt + 1) * 512],
                                    start=(c == 0), stop=(c == 7))
                em.op(TE, mmu, reads=[b_wh0, bxb[c][nt]], writes=[b_pp])

            def evu(e, nt=nt, pp=pp, h=h):
                return e.activation(out=ub[:, nt * 512:(nt + 1) * 512],
                                    in_=pp[:, 0:512], func=AF.Identity,
                                    bias=qbl[:, 24 + h: 25 + h])
            em.op(S, evu, reads=[b_pp, b_qbl], writes=[b_ub])

        for st in range(NCH):
            pp, b_pp = psum[st % 2]
            for c in range(8):
                def mmv(e, c=c, st=st, pp=pp):
                    return e.matmul(pp[:, 0:128],
                                    xbf[:, c * T + st * 128: c * T + (st + 1) * 128],
                                    wh0[:, c * 512 + VO: c * 512 + VO + 128],
                                    start=(c == 0),
                                    stop=(limit == "nok1" and c == 7))
                em.op(TE, mmv, reads=[b_wh0, bxb[c][st // 4]], writes=[b_pp])

            if limit == "nok1":
                pass
            else:
                def mmvb(e, pp=pp, h=h):
                    return e.matmul(pp[:, 0:128], onesrow[:, 0:128],
                                    bvrow[0:1, h * 128:(h + 1) * 128],
                                    start=False, stop=True)
                em.op(TE, mmvb, reads=[b_onesrow, b_bvrow], writes=[b_pp])

            def evv(e, st=st, pp=pp):
                return e.activation(out=vtm[:, st * 128:(st + 1) * 128],
                                    in_=pp[:, 0:128], func=AF.Copy)
            em.op(S, evv, reads=[b_pp], writes=[b_vtm])

        if limit == "h0v" and h == 0:
            stop_dump.append((vtm, b_vtm, T))
            arena.pop("L")
            break

        for src, b_src, dst, b_dst in ((klo, b_klo, klot, b_klot),
                                       (khi, b_khi, khit, b_khit)):
            for st in range(NCH):
                bk, tj = (st // 4) % 2, st % 4
                pbt, b_pbt = psum[bk]

                def tr(e, st=st, tj=tj, src=src, pbt=pbt):
                    return e.matmul(pbt[:, tj * 128:(tj + 1) * 128],
                                    src[:, st * 128:(st + 1) * 128], identb[:],
                                    start=True, stop=True)
                em.op(TE, tr, reads=[b_src, b_identb], writes=[b_pbt])
                if st % 4 == 3:
                    grp = st // 4
                    pbt2, b_pbt2 = psum[grp % 2]

                    def ev(e, grp=grp, dst=dst, pbt2=pbt2):
                        return e.tensor_copy(out=dst[:, grp * 512:(grp + 1) * 512],
                                             in_=pbt2[:, 0:512])
                    em.op(V, ev, reads=[b_pbt2], writes=[b_dst])

        if limit == "h0tp" and h == 0:
            stop_dump.append((klot, b_klot, T))
            arena.pop("L")
            break
        ksnap = b_ksnap = None
        if limit == "snap" and h == 0:
            ksnap, b_ksnap = arena.alloc("ksnap", 128, T, BF16)
            for ss in range(4):
                def snp(e, ss=ss, ksnap=ksnap):
                    return e.tensor_copy(out=ksnap[:, ss * 1024:(ss + 1) * 1024],
                                         in_=klot[:, ss * 1024:(ss + 1) * 1024])
                em.op(V, snp, reads=[b_klot], writes=[b_ksnap])

        if limit == "h0sx5" and h == 0:
            em.barrier()
        _stlist = list(range(NCH))
        _ehlist = ((0, klot, b_klot), (1, khit, b_khit))
        if limit == "h0sx1" and h == 0:
            _ehlist = ((0, vtm, b_vtm), (1, vtm, b_vtm))
        if limit == "h0sx2" and h == 0:
            _ehlist = ((0, klot, b_klot),)
        if limit == "h0sx3" and h == 0:
            _stlist = [0, 1, 2, 3, 16, 17, 18, 19]
        for st in _stlist:
            own = st < 16
            first = (st % 16) == 0
            last = (st % 16) == 15 or (limit == "h0sx3" and h == 0 and st % 16 == 3)
            for eh, kt, b_kt in _ehlist:
                pb, b_pb = psum[(0 if own else 2) + eh]

                def mms(e, st=st, kt=kt, pb=pb, first=first, last=last):
                    return e.matmul(pb[:, 0:128], kt[:, st * 128:(st + 1) * 128],
                                    vtm[:, st * 128:(st + 1) * 128],
                                    start=first, stop=last)
                em.op(TE, mms, reads=[b_kt, b_vtm], writes=[b_pb])

        if limit == "h0sx5" and h == 0:
            stop_dump.append((klot, b_klot, T))
            break
        if limit == "h0sx6" and h == 0:
            pdmp, b_pdmp = arena.alloc("pdmp", 128, 256, BF16)
            def evp6(e, pdmp=pdmp):
                return e.tensor_copy(out=pdmp[:, 0:128], in_=psum_bf[6][0][:, 0:128])
            em.op(V, evp6, reads=[psum_bf[6][1]], writes=[b_pdmp])
            def evp7(e, pdmp=pdmp):
                return e.tensor_copy(out=pdmp[:, 128:256], in_=psum_bf[7][0][:, 0:128])
            em.op(V, evp7, reads=[psum_bf[7][1]], writes=[b_pdmp])
            stop_dump.append((pdmp, b_pdmp, 256))
            break
        if limit == "noalias" and h == 0:
            stop_dump.append((klot, b_klot, T))
            break
        if limit == "snap" and h == 0:
            snap2, b_snap2 = arena.alloc("snap2", 128, 2 * T, BF16)
            for ss in range(4):
                def sn2(e, ss=ss, snap2=snap2):
                    return e.tensor_copy(out=snap2[:, ss * 1024:(ss + 1) * 1024],
                                         in_=ksnap[:, ss * 1024:(ss + 1) * 1024])
                em.op(V, sn2, reads=[b_ksnap], writes=[b_snap2])
                def sn3(e, ss=ss, snap2=snap2):
                    return e.tensor_copy(out=snap2[:, T + ss * 1024: T + (ss + 1) * 1024],
                                         in_=klot[:, ss * 1024:(ss + 1) * 1024])
                em.op(V, sn3, reads=[b_klot], writes=[b_snap2])
            stop_dump.append((snap2, b_snap2, 2 * T))
            break
        if limit in ("h0sx1", "h0sx2", "h0sx3") and h == 0:
            stop_dump.append((klot, b_klot, T))
            break
        if limit == "nok1" and h == 0:
            stop_dump.append((klot, b_klot, T))
            break
        if limit == "mega" and h == 0:
            md, b_md = arena.alloc("mega", 128, T, BF16, "R")
            for q4 in range(4):
                def c1(e, q4=q4, md=md):
                    return e.tensor_copy(out=md[:, q4 * 1024:(q4 + 1) * 1024],
                                         in_=khi[:, q4 * 1024:(q4 + 1) * 1024])
                em.op(V, c1, reads=[b_khi], writes=[b_md])
            stop_dump.append((md, b_md, T))
            break
        if limit in ("h0st2", "h0st3", "h0st4") and h == 0:
            stop_dump.append((klot if limit == "h0st2" else (vtm if limit == "h0st3" else khit),
                              b_klot if limit == "h0st2" else (b_vtm if limit == "h0st3" else b_khit), T))
            break
        if limit == "h0st" and h == 0:
            stdmp, b_stdmp = arena.alloc("stdmp", 128, 512, BF16)
            for i in range(4):
                def evst(e, i=i, stdmp=stdmp):
                    return e.tensor_copy(out=stdmp[:, i * 128:(i + 1) * 128],
                                         in_=psum[i][0][:, 0:128])
                em.op(V, evst, reads=[psum[i][1]], writes=[b_stdmp])
            stop_dump.append((stdmp, b_stdmp, 512))
            break

        for eh in range(2):
            pS, b_pS = psum[2 + eh]
            pO, b_pO = psum[0 + eh]

            def mi1(e, eh=eh, pS=pS, c1_ap=c1_ap):
                return e.tensor_scalar(out=Msb[:, eh * 128:(eh + 1) * 128],
                                       in0=pS[:, 0:128], scalar1=c1_ap,
                                       scalar2=None, op0=ALU.mult)
            em.op(V, mi1, reads=[b_pS, b_smalls], writes=[b_Msb])

            def mi2(e, eh=eh, pO=pO):
                return e.tensor_tensor(out=Msb[:, eh * 128:(eh + 1) * 128],
                                       in0=Msb[:, eh * 128:(eh + 1) * 128],
                                       in1=pO[:, 0:128], op=ALU.add)
            em.op(V, mi2, reads=[b_pO, b_Msb], writes=[b_Msb])
        em.op(V, lambda e: e.tensor_copy(out=Mbf[:, 0:256], in_=Msb[:, 0:256]),
              reads=[b_Msb], writes=[b_Mbf])

        for g in range(NG):
            pk, b_pk = psum[2 + g % 2]
            for eh, kt, b_kt in ((0, klot, b_klot), (1, khit, b_khit)):
                for j in range(2):
                    st = 2 * g + j

                    def mmkv(e, st=st, eh=eh, kt=kt, pk=pk, j=j):
                        return e.matmul(pk[:, eh * 128:(eh + 1) * 128],
                                        kt[:, st * 128:(st + 1) * 128],
                                        vtm[:, st * 128:(st + 1) * 128],
                                        start=(j == 0), stop=(j == 1))
                    em.op(TE, mmkv, reads=[b_kt, b_vtm], writes=[b_pk])
            if g < NG - 1:
                def mup(e, g=g, pk=pk, al_ap=al_ap):
                    return e.scalar_tensor_tensor(
                        out=Msb[:, (g + 1) * 256:(g + 2) * 256],
                        in0=pk[:, 0:256], scalar=al_ap,
                        in1=Msb[:, g * 256:(g + 1) * 256],
                        op0=ALU.mult, op1=ALU.add)
                em.op(V, mup, reads=[b_pk, b_Msb, b_smalls], writes=[b_Msb])

                def mcast(e, g=g):
                    return e.tensor_copy(out=Mbf[:, (g + 1) * 256:(g + 2) * 256],
                                         in_=Msb[:, (g + 1) * 256:(g + 2) * 256])
                em.op(V, mcast, reads=[b_Msb], writes=[b_Mbf])

        if limit == "h0m" and h == 0:
            stop_dump.append((Mbf, b_Mbf, NG * 256))
            arena.pop("L")
            break

        g0stage = 99
        if limit and limit.startswith("h0s") and h == 0 and limit[3:].isdigit():
            g0stage = int(limit[3:])
        ngrun = 1 if (limit == "h0g1" or (limit or "").startswith("h0s")) and h == 0 else NG
        for g in range(ngrun):
            qc0, qc1 = 2 * g, 2 * g + 1
            pe, b_pe = psum[4 + g % 2]
            po, b_po = psum[6 + g % 2]
            for eh, kf, b_kf, qf, b_qf in ((0, klo, b_klo, qlo, b_qlo),
                                           (1, khi, b_khi, qhi, b_qhi)):
                def mme0(e, kf=kf, qf=qf, pe=pe, g=g, qc0=qc0, eh=eh):
                    return e.matmul(pe[:, 0:256],
                                    kf[:, qc0 * 128:(qc0 + 1) * 128],
                                    qf[:, g * 256:(g + 1) * 256],
                                    start=(eh == 0), stop=(eh == 1))
                em.op(TE, mme0, reads=[b_kf, b_qf], writes=[b_pe])
            for eh, kf, b_kf, qf, b_qf in ((0, klo, b_klo, qlo, b_qlo),
                                           (1, khi, b_khi, qhi, b_qhi)):
                def mme1(e, kf=kf, qf=qf, pe=pe, qc1=qc1, eh=eh):
                    return e.matmul(pe[:, 256:384],
                                    kf[:, qc1 * 128:(qc1 + 1) * 128],
                                    qf[:, qc1 * 128:(qc1 + 1) * 128],
                                    start=(eh == 0), stop=(eh == 1))
                em.op(TE, mme1, reads=[b_kf, b_qf], writes=[b_pe])
            if g0stage < 2:
                continue
            em.op(V, lambda e, pe=pe: e.tensor_tensor(out=etsb[:, 0:128],
                                                      in0=pe[:, 0:128], in1=maskh[:],
                                                      op=ALU.mult),
                  reads=[b_pe, b_maskh], writes=[b_etsb])
            em.op(S, lambda e, pe=pe, al_ap=al_ap: e.activation(
                out=etsb[:, 128:256], in_=pe[:, 128:256], func=AF.Identity,
                scale=al_ap),
                reads=[b_pe, b_smalls], writes=[b_etsb])
            em.op(V, lambda e, pe=pe: e.tensor_tensor(out=etsb[:, 256:384],
                                                      in0=pe[:, 256:384], in1=maskh[:],
                                                      op=ALU.mult),
                  reads=[b_pe, b_maskh], writes=[b_etsb])

            if g0stage < 3:
                continue

            def mmx0(e, po=po, g=g):
                return e.matmul(po[:, 0:256], Mbf[:, g * 256: g * 256 + 128],
                                qlo[:, g * 256:(g + 1) * 256], start=True, stop=False)
            em.op(TE, mmx0, reads=[b_Mbf, b_qlo], writes=[b_po])

            def mmx1(e, po=po, g=g):
                return e.matmul(po[:, 0:256], Mbf[:, g * 256 + 128:(g + 1) * 256],
                                qhi[:, g * 256:(g + 1) * 256], start=False, stop=False)
            em.op(TE, mmx1, reads=[b_Mbf, b_qhi], writes=[b_po])

            if g0stage < 4:
                def mmx1b(e, po=po, g=g):
                    return e.matmul(po[:, 0:256], Mbf[:, g * 256 + 128:(g + 1) * 256],
                                    qhi[:, g * 256:(g + 1) * 256], start=False,
                                    stop=True)
                em.op(TE, mmx1b, reads=[b_Mbf, b_qhi], writes=[b_po])
                continue

            def mmi0(e, po=po, qc0=qc0):
                return e.matmul(po[:, 0:256], vtm[:, qc0 * 128:(qc0 + 1) * 128],
                                etsb[:, 0:256], start=False, stop=False)
            em.op(TE, mmi0, reads=[b_vtm, b_etsb], writes=[b_po])

            def mmi1(e, po=po, qc1=qc1):
                return e.matmul(po[:, 128:256], vtm[:, qc1 * 128:(qc1 + 1) * 128],
                                etsb[:, 256:384], start=False, stop=True)
            em.op(TE, mmi1, reads=[b_vtm, b_etsb], writes=[b_po])

            if g0stage < 5:
                continue

            def evg(e, po=po, g=g, h=h):
                return e.tensor_tensor(
                    out=gated[:, h * TOWN + g * 256: h * TOWN + (g + 1) * 256],
                    in0=po[:, 0:256], in1=ub[:, g * 256:(g + 1) * 256], op=ALU.mult)
            em.op(V, evg, reads=[b_po, b_ub], writes=[b_gated])

            if g0stage < 6:
                continue

            def evsq(e, po=po, g=g):
                return e.activation(out=asq[:, (g % 2) * 256:(g % 2) * 256 + 256],
                                    in_=po[:, 0:256], func=AF.Square)
            em.op(S, evsq, reads=[b_po], writes=[b_asq])

            if g0stage < 7:
                continue
            pr, b_pr = psum[0 + g % 2]

            def mmr(e, g=g, pr=pr):
                return e.matmul(pr[:, 0:256], oneskd[:],
                                asq[:, (g % 2) * 256:(g % 2) * 256 + 256],
                                start=True, stop=True)
            em.op(TE, mmr, reads=[b_oneskd, b_asq], writes=[b_pr])

            def accr(e, g=g, pr=pr):
                return e.tensor_tensor(out=rmsac[:, g * 256:(g + 1) * 256],
                                       in0=rmsac[:, g * 256:(g + 1) * 256],
                                       in1=pr[:, 0:256], op=ALU.add)
            em.op(V, accr, reads=[b_pr, b_rmsac], writes=[b_rmsac])

        if (limit in ("h0et", "h0g1")) and h == 0:
            stop_dump.append((gated, b_gated, 8 * TOWN))
            arena.pop("L")
            break

        arena.pop("L")

        if limit == "head0" and h == 0:
            break
        if h + 1 < H:
            dma(wh0[:, :], wh_h[h + 1, :, :], [], [b_wh0])

    if stop_dump:
        vv, bb, ww = stop_dump[0]
        em.barrier()
        dma(dbg_h[:, 0:ww], vv[:, 0:ww], [bb], [], force=True)
        em.final_wait(SY, ["d_" + bb.name])
        return finish()
    if limit == "head0":
        em.barrier()
        dma(dbg_h[:, 0:8 * TOWN], gated[:, 0:8 * TOWN], [b_gated], [], force=True)
        em.final_wait(SY, ["d_gated"])
        return finish()
    arena.pop("L")
    em.barrier()

    em.op(S, lambda e: e.activation(out=rmsac[:, :], in_=rmsac[:, :], func=AF.Ln,
                                    bias=epsrms[:, 0:1]),
          reads=[b_epsrms, b_rmsac], writes=[b_rmsac])
    em.op(S, lambda e: e.activation(out=rmsac[:, :], in_=rmsac[:, :], func=AF.Exp,
                                    scale=-0.5),
          reads=[b_rmsac], writes=[b_rmsac])

    arena.push("L")
    r1, b_r1 = arena.alloc("r1", 128, 8 * TOWN, F32)
    wf, b_wf = arena.alloc("wf", 128, 8192, BF16)
    wp, b_wp = arena.alloc("wp", 128, 2048, BF16)
    arena.push("L")
    wo, b_wo = arena.alloc("wo", 128, 8192, BF16)
    xf0, b_xf0 = arena.alloc("xf0", 128, TOWN, F32)
    xf1, b_xf1 = arena.alloc("xf1", 128, TOWN, F32)
    tmpo, b_tmpo = arena.alloc("tmpo", 128, 512, F32)

    dma(wo[:, :], wo_h[:, :], [], [b_wo])
    dma(wf[:, :], wf_h[:, :], [], [b_wf])
    dma(wp[:, :], wp_h[:, :], [], [b_wp])
    dma(wd[:], wd_h[:, :], [], [b_wd])
    dma(wu[:], wu_h[:, :], [], [b_wu])

    xfs = [(xf0, b_xf0), (xf1, b_xf1)]
    for oc in range(8):
        xf, b_xf = xfs[oc % 2]
        dma(xf[:, :], xf32_h[oc, :, :], [], [b_xf])
        for nt in range(4):
            pp, b_pp = psum[nt % 2]
            for c in range(8):
                def mmo(e, c=c, oc=oc, nt=nt, pp=pp):
                    return e.matmul(pp[:, 0:512],
                                    wo[:, c * 1024 + oc * 128: c * 1024 + (oc + 1) * 128],
                                    gated[:, c * TOWN + nt * 512: c * TOWN + (nt + 1) * 512],
                                    start=(c == 0), stop=(c == 7))
                em.op(TE, mmo, reads=[b_wo, b_gated], writes=[b_pp])

            def evo1(e, nt=nt, pp=pp):
                return e.tensor_tensor(out=tmpo[:, 0:512], in0=pp[:, 0:512],
                                       in1=rmsac[:, nt * 512:(nt + 1) * 512],
                                       op=ALU.mult)
            em.op(V, evo1, reads=[b_pp, b_rmsac], writes=[b_tmpo])

            def evo2(e, oc=oc, nt=nt, xf=xf):
                return e.scalar_tensor_tensor(
                    out=r1[:, oc * TOWN + nt * 512: oc * TOWN + (nt + 1) * 512],
                    in0=tmpo[:, 0:512], scalar=obl[:, oc:oc + 1],
                    in1=xf[:, nt * 512:(nt + 1) * 512], op0=ALU.add, op1=ALU.add)
            em.op(V, evo2, reads=[b_tmpo, b_obl, b_xf], writes=[b_r1])

    arena.pop("L")
    arena.pop("R")
    em.barrier()

    arena.push("R")
    xc2, b_xc2 = arena.alloc("xc2", 128, 8 * TOWN, BF16, "R")
    varsb, b_varsb = arena.alloc("varsb4", 128, 512, F32, "R")
    var2sb, b_var2sb = arena.alloc("var2sb4", 128, 512, F32, "R")
    rsb, b_rsb = arena.alloc("rsb4", 128, 512, F32, "R")
    t1sb, b_t1sb = arena.alloc("t1sb4", 128, 512, F32, "R")
    xsqs = [arena.alloc("xsq4_%d" % i, 128, 512, BF16, "R") for i in range(2)]
    rbfs = [arena.alloc("rbf4_%d" % i, 128, 512, BF16, "R") for i in range(2)]

    for grp in range(4):
        g0 = grp * 512
        pm, b_pm = psum[2 + grp % 2]
        ps2, b_ps2 = psum[4 + grp % 2]
        for c in range(8):
            rbf, b_rbf = rbfs[c % 2]

            def cst(e, c=c, g0=g0, rbf=rbf):
                return e.activation(out=rbf[:, 0:512], func=AF.Copy,
                                    in_=r1[:, c * TOWN + g0: c * TOWN + g0 + 512])
            em.op(S, cst, reads=[b_r1], writes=[b_rbf])

            def mm_m(e, c=c, rbf=rbf, pm=pm):
                return e.matmul(pm[:, 0:512], oneskd[:], rbf[:, 0:512],
                                start=(c == 0), stop=(c == 7))
            em.op(TE, mm_m, reads=[b_oneskd, b_rbf], writes=[b_pm])
        for c in range(8):
            xsq, b_xsq = xsqs[c % 2]

            def sq(e, c=c, g0=g0, xsq=xsq):
                return e.activation(out=xsq[:, 0:512], func=AF.Square,
                                    in_=r1[:, c * TOWN + g0: c * TOWN + g0 + 512])
            em.op(S, sq, reads=[b_r1], writes=[b_xsq])

            def mm_s2(e, c=c, xsq=xsq, ps2=ps2):
                return e.matmul(ps2[:, 0:512], oneskd[:], xsq[:, 0:512],
                                start=(c == 0), stop=(c == 7))
            em.op(TE, mm_s2, reads=[b_oneskd, b_xsq], writes=[b_ps2])
        em.op(S, lambda e, pm=pm: e.activation(out=varsb[:, 0:512],
                                               in_=pm[:, 0:512], func=AF.Square),
              reads=[b_pm], writes=[b_varsb])
        em.op(V, lambda e, ps2=ps2: e.tensor_tensor(out=var2sb[:, 0:512],
                                                    in0=ps2[:, 0:512],
                                                    in1=varsb[:, 0:512],
                                                    op=ALU.subtract),
              reads=[b_ps2, b_varsb], writes=[b_var2sb])
        em.op(S, lambda e: e.activation(out=var2sb[:, 0:512], in_=var2sb[:, 0:512],
                                        func=AF.Ln, bias=epsln[:, 0:1]),
              reads=[b_epsln, b_var2sb], writes=[b_var2sb])
        em.op(S, lambda e: e.activation(out=rsb[:, 0:512], in_=var2sb[:, 0:512],
                                        func=AF.Exp, scale=-0.5),
              reads=[b_var2sb], writes=[b_rsb])
        for c in range(8):
            def sub(e, c=c, g0=g0, pm=pm):
                return e.tensor_tensor(out=t1sb[:, 0:512],
                                       in0=r1[:, c * TOWN + g0: c * TOWN + g0 + 512],
                                       in1=pm[:, 0:512], op=ALU.subtract)
            em.op(V, sub, reads=[b_r1, b_pm], writes=[b_t1sb])

            def mul(e, c=c, g0=g0):
                return e.tensor_tensor(out=xc2[:, c * TOWN + g0: c * TOWN + g0 + 512],
                                       in0=t1sb[:, 0:512], in1=rsb[:, 0:512],
                                       op=ALU.mult)
            em.op(V, mul, reads=[b_t1sb, b_rsb], writes=[b_xc2])

    zsb, b_zsb = arena.alloc("zsb", 128, 2 * TOWN, F32, "R")
    zsq4 = [arena.alloc("zsq4_%d" % i, 128, 512, BF16, "R") for i in range(2)]
    rn4 = [arena.alloc("rn4_%d" % i, 128, 512, F32, "R") for i in range(2)]
    pjb, b_pjb = arena.alloc("pjb", 128, 2 * TOWN, BF16, "R")

    for pc in range(2):
        for nt in range(4):
            pp, b_pp = psum[nt % 2]
            for c in range(8):
                def mmp(e, c=c, pc=pc, nt=nt, pp=pp):
                    return e.matmul(pp[:, 0:512],
                                    wp[:, c * 256 + pc * 128: c * 256 + (pc + 1) * 128],
                                    xc2[:, c * TOWN + nt * 512: c * TOWN + (nt + 1) * 512],
                                    start=(c == 0), stop=(c == 7))
                em.op(TE, mmp, reads=[b_wp, b_xc2], writes=[b_pp])

            def evp(e, pc=pc, nt=nt, pp=pp):
                return e.activation(
                    out=zsb[:, pc * TOWN + nt * 512: pc * TOWN + (nt + 1) * 512],
                    in_=pp[:, 0:512], func=AF.Identity, bias=bpl[:, pc:pc + 1])
            em.op(S, evp, reads=[b_pp, b_bpl], writes=[b_zsb])
    for nt in range(4):
        pp, b_pp = psum[2 + nt % 2]
        for pc in range(2):
            zq, b_zq = zsq4[pc]

            def sqz(e, pc=pc, nt=nt, zq=zq):
                return e.activation(out=zq[:, 0:512], func=AF.Square,
                                    in_=zsb[:, pc * TOWN + nt * 512: pc * TOWN + (nt + 1) * 512])
            em.op(S, sqz, reads=[b_zsb], writes=[b_zq])

            def mmn(e, pc=pc, zq=zq, pp=pp):
                return e.matmul(pp[:, 0:512], onesb[:], zq[:, 0:512],
                                start=(pc == 0), stop=(pc == 1))
            em.op(TE, mmn, reads=[b_onesb, b_zq], writes=[b_pp])
        rn, b_rn = rn4[nt % 2]

        def evn(e, pp=pp, rn=rn):
            return e.activation(out=rn[:, 0:512], in_=pp[:, 0:512], func=AF.Ln,
                                bias=epsl2[:, 0:1])
        em.op(S, evn, reads=[b_pp, b_epsl2], writes=[b_rn])
        em.op(S, lambda e, rn=rn: e.activation(out=rn[:, 0:512], in_=rn[:, 0:512],
                                               func=AF.Exp, scale=-0.5),
              reads=[b_rn], writes=[b_rn])
        for pc in range(2):
            eng = V

            def pj(e, pc=pc, nt=nt, rn=rn):
                return e.tensor_tensor(
                    out=pjb[:, pc * TOWN + nt * 512: pc * TOWN + (nt + 1) * 512],
                    in0=zsb[:, pc * TOWN + nt * 512: pc * TOWN + (nt + 1) * 512],
                    in1=rn[:, 0:512], op=ALU.mult)
            em.op(eng, pj, reads=[b_zsb, b_rn], writes=[b_pjb])

    for st in range(16):
        pp, b_pp = psum[4 + st % 2]
        for pc in range(2):
            def mml(e, pc=pc, st=st, pp=pp):
                return e.matmul(pp[:, 0:4],
                                pjb[:, pc * TOWN + st * 128: pc * TOWN + (st + 1) * 128],
                                simn[:, pc * 4:(pc + 1) * 4],
                                start=(pc == 0), stop=(pc == 1))
            em.op(TE, mml, reads=[b_pjb, b_simn], writes=[b_pp])
        em.op(V, lambda e, pp=pp: e.reduce_max(out=mxt[:], in_=pp[:, 0:4], axis=AXL.X),
              reads=[b_pp], writes=[b_mxt])
        em.op(V, lambda e, pp=pp: e.tensor_tensor(
            out=dvt[:], in0=pp[:, 0:4],
            in1=mxt[:, 0:1].broadcast_to([128, 4]), op=ALU.subtract),
            reads=[b_pp, b_mxt], writes=[b_dvt])
        em.op(S, lambda e: e.sign(out=sgt[:], in_=dvt[:]),
              reads=[b_dvt], writes=[b_sgt])
        em.op(V, lambda e: e.tensor_scalar(out=bet[:], in0=mxt[:],
                                           scalar1=smalls[:, 16:17], scalar2=None,
                                           op0=ALU.mult),
              reads=[b_mxt, b_smalls], writes=[b_bet])

        def evex(e, pp=pp):
            return e.activation(out=ext[:], in_=pp[:, 0:4], func=AF.Exp,
                                scale=smalls[:, 17:18], bias=bet[:, 0:1],
                                accum_out=sut[:])
        em.op(S, evex, reads=[b_pp, b_bet, b_smalls], writes=[b_ext, b_sut])
        em.op(V, lambda e: e.reciprocal(out=sut[:], in_=sut[:]),
              reads=[b_sut], writes=[b_sut])
        em.op(V, lambda e: e.tensor_scalar(out=lwt[:], in0=ext[:],
                                           scalar1=sut[:, 0:1], scalar2=None,
                                           op0=ALU.mult),
              reads=[b_ext, b_sut], writes=[b_lwt])
        em.op(V, lambda e: e.scalar_tensor_tensor(out=lwt[:], in0=sgt[:], scalar=1.0,
                                                  in1=lwt[:], op0=ALU.add,
                                                  op1=ALU.mult),
              reads=[b_sgt, b_lwt], writes=[b_lwt])
        em.op(V, lambda e: e.tensor_copy(out=rtt[:, 0:1], in_=lwt[:, 0:1]),
              reads=[b_lwt], writes=[b_rtt])
        em.op(V, lambda e: e.tensor_copy(out=rtt[:, 1:3],
                                         in_=lwt[:, 1:2].broadcast_to([128, 2])),
              reads=[b_lwt], writes=[b_rtt])
        em.op(V, lambda e: e.tensor_copy(out=rtt[:, 3:7],
                                         in_=lwt[:, 2:3].broadcast_to([128, 4])),
              reads=[b_lwt], writes=[b_rtt])
        em.op(V, lambda e: e.tensor_copy(out=rtt[:, 7:15],
                                         in_=lwt[:, 3:4].broadcast_to([128, 8])),
              reads=[b_lwt], writes=[b_rtt])

        def trr(e):
            return e.transpose(psum[6][0][0:15, 0:128], rtt[:, 0:15], identf[:])
        em.op(TE, trr, reads=[b_rtt, b_identf], writes=[psum[6][1]])

        def evr(e, st=st):
            return e.tensor_copy(out=ret15[0:15, st * 128:(st + 1) * 128],
                                 in_=psum[6][0][0:15, 0:128])
        em.op(V, evr, reads=[psum[6][1]], writes=[b_ret15])

    for nt in range(4):
        pp, b_pp = psum[2 + nt % 2]
        for c in range(8):
            def mmd(e, c=c, nt=nt, pp=pp):
                return e.matmul(pp[0:15, 0:512], wd[:, c * 15:(c + 1) * 15],
                                xc2[:, c * TOWN + nt * 512: c * TOWN + (nt + 1) * 512],
                                start=(c == 0), stop=(c == 7))
            em.op(TE, mmd, reads=[b_wd, b_xc2], writes=[b_pp])

        def evh(e, nt=nt, pp=pp):
            return e.tensor_tensor(out=hpb[0:15, nt * 512:(nt + 1) * 512],
                                   in0=pp[0:15, 0:512],
                                   in1=ret15[0:15, nt * 512:(nt + 1) * 512],
                                   op=ALU.mult)
        em.op(V, evh, reads=[b_pp, b_ret15], writes=[b_hpb])

    ost0, b_ost0 = arena.alloc("ost0", 128, TOWN, F32, "R")
    ost1, b_ost1 = arena.alloc("ost1", 128, TOWN, F32, "R")
    osts = [(ost0, b_ost0), (ost1, b_ost1)]
    for oc in range(8):
        ost, b_ost = osts[oc % 2]
        for nt in range(4):
            pp, b_pp = psum[nt % 2]
            for c in range(8):
                def mmb(e, c=c, oc=oc, nt=nt, pp=pp):
                    return e.matmul(pp[:, 0:512],
                                    wf[:, c * 1024 + oc * 128: c * 1024 + (oc + 1) * 128],
                                    xc2[:, c * TOWN + nt * 512: c * TOWN + (nt + 1) * 512],
                                    start=(c == 0), stop=False)
                em.op(TE, mmb, reads=[b_wf, b_xc2], writes=[b_pp])

            def mmup(e, oc=oc, nt=nt, pp=pp):
                return e.matmul(pp[:, 0:512], wu[:, oc * 128:(oc + 1) * 128],
                                hpb[0:15, nt * 512:(nt + 1) * 512],
                                start=False, stop=True)
            em.op(TE, mmup, reads=[b_wu, b_hpb], writes=[b_pp])

            def evf(e, oc=oc, nt=nt, pp=pp, ost=ost):
                return e.tensor_tensor(
                    out=ost[:, nt * 512:(nt + 1) * 512], in0=pp[:, 0:512],
                    in1=r1[:, oc * TOWN + nt * 512: oc * TOWN + (nt + 1) * 512],
                    op=ALU.add)
            em.op(V, evf, reads=[b_pp, b_r1], writes=[b_ost])
        dma(out_h[oc, :, :], ost[:, :], [b_ost], [], force=True)

    arena.pop("R")
    arena.pop("L")

    em.final_wait(SY, ["d_ost0", "d_ost1"])
    return finish()


def make_inputs(ins):
    x = np.asarray(ins["L0_attn_x"], dtype=np.float32)
    qkvu_w = np.asarray(ins["L0_attn_qkvu_w"], dtype=np.float32)
    qkvu_b = np.asarray(ins["L0_attn_qkvu_b"], dtype=np.float32)
    out_w = np.asarray(ins["L0_attn_out_w"], dtype=np.float32)
    out_b = np.asarray(ins["L0_attn_out_b"], dtype=np.float32)
    theta = np.asarray(ins["L0_attn_theta"], dtype=np.float32)
    slope = np.asarray(ins["L0_attn_slope_rate"], dtype=np.float32)
    W = np.asarray(ins["L0_ffn_W"], dtype=np.float32)
    Wp = np.asarray(ins["L0_ffn_Wp"], dtype=np.float32)
    bp = np.asarray(ins["L0_ffn_bp"], dtype=np.float32)
    sim = np.asarray(ins["L0_ffn_sim"], dtype=np.float32)
    temp = np.asarray(ins["L0_ffn_temp"], dtype=np.float32)
    Wd = np.concatenate([np.asarray(ins["L0_ffn_Wd%d" % i], dtype=np.float32)
                         for i in range(4)], axis=0)
    Wu = np.concatenate([np.asarray(ins["L0_ffn_Wu%d" % i], dtype=np.float32)
                         for i in range(4)], axis=1)

    A = qkvu_w.reshape(4, 8, 128, 8, 128)
    w_qkvu = np.ascontiguousarray(
        A.transpose(1, 4, 3, 0, 2).reshape(8, 128, 4096)).astype(BF)
    w_out = np.ascontiguousarray(
        out_w.T.reshape(8, 128, 1024).transpose(1, 0, 2).reshape(128, 8192)).astype(BF)
    w_ffn = np.ascontiguousarray(
        W.T.reshape(8, 128, 1024).transpose(1, 0, 2).reshape(128, 8192)).astype(BF)
    w_proj = np.ascontiguousarray(
        Wp.T.reshape(8, 128, 256).transpose(1, 0, 2).reshape(128, 2048)).astype(BF)
    w_down = np.ascontiguousarray(
        Wd.T.reshape(8, 128, 15).transpose(1, 0, 2).reshape(128, 120)).astype(BF)
    w_up = np.ascontiguousarray(Wu.T).astype(BF)
    theta_lay = np.ascontiguousarray(theta.reshape(8, 128).T)
    qkvu_b_lay = np.ascontiguousarray(qkvu_b.reshape(32, 128).T)
    v_b_row = np.ascontiguousarray(qkvu_b[2048:3072].reshape(1, 1024)).astype(BF)
    out_b_lay = np.ascontiguousarray(out_b.reshape(8, 128).T)
    bp_lay = np.ascontiguousarray(bp.reshape(2, 128).T)
    sim_t = np.ascontiguousarray(sim.T)
    slope_row = np.ascontiguousarray(slope.reshape(1, 8))
    temp_row = np.ascontiguousarray(temp.reshape(1, 1))

    shared = dict(w_qkvu=w_qkvu, w_out=w_out, w_ffn=w_ffn, w_proj=w_proj,
                  w_down=w_down, w_up=w_up, theta_lay=theta_lay,
                  qkvu_b_lay=qkvu_b_lay, v_b_row=v_b_row, out_b_lay=out_b_lay,
                  bp_lay=bp_lay, sim_t=sim_t, slope_row=slope_row,
                  temp_row=temp_row)

    in_maps = []
    for core in range(NCORES):
        b = core // 2
        parity = core % 2
        off = TOWN * parity
        xt = x[b].T
        own = xt[:, off:off + TOWN]
        part = xt[:, TOWN:2 * TOWN] if parity == 0 else xt[:, 0:TOWN]
        xr = np.concatenate([own, part], axis=1)
        xbf = np.ascontiguousarray(xr.reshape(8, 128, T)).astype(BF)
        xf32 = np.ascontiguousarray(own.reshape(8, 128, TOWN)).astype(np.float32)
        aux_row = np.array([[float(parity), float(off), float(-off), 0.0]],
                           dtype=np.float32)
        m = dict(shared)
        m["xbf"] = xbf
        m["xf32"] = xf32
        m["aux_row"] = aux_row
        in_maps.append(m)
    return in_maps


_NC_CACHE = None


def kernel(**inputs):
    global _NC_CACHE
    in_maps = make_inputs(inputs)
    if _NC_CACHE is None:
        _NC_CACHE = build_program()
    res = run_bass_kernel_spmd(_NC_CACHE, in_maps, core_ids=list(range(NCORES)))
    out = np.empty((B, T, DM), dtype=np.float32)
    for core in range(NCORES):
        b = core // 2
        off = TOWN * (core % 2)
        o = res.results[core]["out"].reshape(DM, TOWN)
        out[b, off:off + TOWN, :] = o.T
    return out


# version 07 (final)
# speedup vs baseline: 1.2819x; 1.2819x over previous
import contextlib
import numpy as np
import ml_dtypes
import concourse.bass as bass
import concourse.mybir as mybir
from concourse.bass_utils import run_bass_kernel_spmd

F32 = mybir.dt.float32
BF16 = mybir.dt.bfloat16
AF = mybir.ActivationFunctionType
ALU = mybir.AluOpType
AXL = mybir.AxisListType

PI = float(np.pi)
MAGIC = 12582912.0
CLAMP_MAX = float(np.log(1.0 / 0.01))
BF = ml_dtypes.bfloat16

B, T, DM = 4, 4096, 1024
H = 8
TOWN = 2048
NCORES = 8
NCH = 32
NG = 8

ENGS = ("sync", "tensor", "vector", "scalar", "gpsimd")
SY, TE, V, S, G_ = ENGS


class Buf:
    __slots__ = ("name", "writer", "readers", "exclusive")

    def __init__(self, name, seed=(), exclusive=False):
        self.name = name
        self.writer = None
        self.readers = list(seed)
        self.exclusive = exclusive


class Em:

    def __init__(self):
        self.recs = {e: [] for e in ENGS}
        self.counts = {}
        self.waited = {e: {} for e in ENGS}
        self.free_pool = []
        self.forced = set()
        self.dma_sems = set()

    def op(self, eng, fn, reads=(), writes=(), deps=(), inc=1, force=False,
           sem=None):
        sem = sem or eng
        if sem != eng:
            self.dma_sems.add(sem)
        toks = list(deps)
        for b in reads:
            if b.writer is not None:
                toks.append(b.writer)
            if b.exclusive:
                toks.extend(b.readers)
        for b in writes:
            if b.writer is not None:
                toks.append(b.writer)
            toks.extend(b.readers)
        need = {}
        for (sm, vv) in toks:
            if sm == eng == TE:
                continue
            if vv > need.get(sm, -1):
                need[sm] = vv
        waits = []
        for sm in sorted(need):
            vv = need[sm]
            if vv > self.waited[eng].get(sm, 0):
                waits.append((sm, vv))
                self.waited[eng][sm] = vv
        self.counts[sem] = self.counts.get(sem, 0) + inc
        tok = (sem, self.counts[sem])
        if force:
            self.forced.add(tok)
        self.recs[eng].append(("call", fn, waits, inc, self.counts[sem], sem))
        for b in reads:
            b.readers.append(tok)
        for b in writes:
            b.writer = tok
            b.readers = []
        return tok

    def barrier(self):
        for sm in sorted(self.dma_sems):
            v = self.counts.get(sm, 0)
            if v > self.waited[SY].get(sm, 0):
                self.recs[SY].append(("wait", sm, v))
                self.waited[SY][sm] = v
        vals = {}
        for e in ENGS:
            v = self.counts.get(e, 0)
            vals[e] = v
            if v:
                self.forced.add((e, v))
        for e in ENGS:
            for o in ENGS:
                if o != e and vals[o] > self.waited[e].get(o, 0):
                    self.recs[e].append(("wait", o, vals[o]))
                    self.waited[e][o] = vals[o]
        self.free_pool = []

    def final_wait(self, eng, sems_to_wait):
        for sm in sems_to_wait:
            self.recs[eng].append(("wait", sm, self.counts.get(sm, 0)))

    DENSE = False

    def finalize(self):
        needed = {}
        if Em.DENSE:
            for e in ENGS:
                for rec in self.recs[e]:
                    if rec[0] == "call":
                        needed.setdefault(rec[5], set()).add(rec[4])
                    elif rec[0] == "inc":
                        needed.setdefault(e, set()).add(rec[1])
        for (sm, v) in self.forced:
            needed.setdefault(sm, set()).add(v)
        for e in ENGS:
            for rec in self.recs[e]:
                if rec[0] == "wait":
                    needed.setdefault(rec[1], set()).add(rec[2])
                elif rec[0] == "call":
                    for (sm, vv) in rec[2]:
                        needed.setdefault(sm, set()).add(vv)
        remap = {}
        rv = {}
        for e in ENGS:
            new = []
            for rec in self.recs[e]:
                if rec[0] == "inc":
                    p = rec[1]
                    if p in needed.get(e, ()):
                        r = rv.get(e, 0) + 1
                        rv[e] = r
                        remap.setdefault(e, {})[p] = r
                        new.append(("inc",))
                elif rec[0] == "call":
                    _, fn, waits, inc, pval, sem = rec
                    if pval in needed.get(sem, ()):
                        r = rv.get(sem, 0) + inc
                        rv[sem] = r
                        remap.setdefault(sem, {})[pval] = r
                        new.append(("call", fn, waits, inc, sem))
                    else:
                        new.append(("call", fn, waits, 0, sem))
                else:
                    new.append(rec)
            self.recs[e] = new
        for e in ENGS:
            out = []
            for rec in self.recs[e]:
                if rec[0] == "wait":
                    out.append(("wait", rec[1], remap[rec[1]][rec[2]]))
                elif rec[0] == "call":
                    _, fn, waits, inc, sem = rec
                    out.append(("call", fn,
                                [(sm, remap[sm][vv]) for (sm, vv) in waits],
                                inc, sem))
                else:
                    out.append(rec)
            self.recs[e] = out

    TE_FENCE = None

    def replay(self, eng_name, eng_obj, sems):
        own = sems[eng_name]
        for rec in self.recs[eng_name]:
            k = rec[0]
            if k == "wait":
                eng_obj.wait_ge(sems[rec[1]], rec[2])
            elif k == "inc":
                eng_obj.drain(fusable=False).then_inc(own, 1)
            else:
                _, fn, waits, inc, sem = rec
                for (sm, vv) in waits:
                    eng_obj.wait_ge(sems[sm], vv)
                inst = fn(eng_obj)
                if inc:
                    if eng_name != "sync" and Em.TE_FENCE:
                        eng_obj.drain(fusable=False).then_inc(sems[sem], inc)
                    else:
                        inst.then_inc(sems[sem], inc)


class View:
    __slots__ = ("ar", "woff", "P", "N", "dt")

    def __init__(self, ar, woff, P, N, dt):
        self.ar, self.woff, self.P, self.N, self.dt = ar, woff, P, N, dt

    def __getitem__(self, idx):
        ps, cs = idx
        c0 = cs.start if cs.start is not None else 0
        c1 = cs.stop if cs.stop is not None else self.N
        if self.dt == F32:
            return self.ar.tf[ps, self.woff + c0: self.woff + c1]
        return self.ar.tb[ps, 2 * self.woff + c0: 2 * self.woff + c1]


class Arena:
    def __init__(self, nc, em, words):
        self.tf = nc.alloc_sbuf_tensor("arena", [128, words], F32)
        self.tb = self.tf.bitcast(BF16)
        self.words = words
        self.em = em
        self.lo = 0
        self.hi = words
        self.stacks = {"L": [], "R": []}

    def push(self, side):
        self.stacks[side].append((self.lo if side == "L" else self.hi, []))

    def pop(self, side):
        mark, bufs = self.stacks[side].pop()
        for b in bufs:
            if b.writer is not None:
                self.em.free_pool.append(b.writer)
            self.em.free_pool.extend(b.readers)
        if side == "L":
            self.lo = mark
        else:
            self.hi = mark

    def alloc(self, name, P, N, dt, side="L"):
        words = N if dt == F32 else (N + 1) // 2
        words = (words + 15) & ~15
        if side == "L":
            off = self.lo
            self.lo += words
        else:
            self.hi -= words
            off = self.hi
        assert self.lo <= self.hi, "arena overflow at %s (%d > %d)" % (
            name, self.lo, self.hi)
        buf = Buf(name, seed=self.em.free_pool)
        if self.stacks[side]:
            self.stacks[side][-1][1].append(buf)
        return View(self, off, P, N, dt), buf


def build_program(limit=None):
    nc = bass.Bass("TRN2", target_bir_lowering=False, debug=False)
    em = Em()

    def din(name, shape, dt=F32):
        return nc.dram_tensor(name, list(shape), dt, kind="ExternalInput").ap()

    xbf_h = din("xbf", [8, 128, T], BF16)
    xf32_h = din("xf32", [8, 128, TOWN], F32)
    wh_h = din("w_qkvu", [H, 128, 4096], BF16)
    wo_h = din("w_out", [128, 8192], BF16)
    wf_h = din("w_ffn", [128, 8192], BF16)
    wp_h = din("w_proj", [128, 2048], BF16)
    wd_h = din("w_down", [128, 120], BF16)
    wu_h = din("w_up", [15, 1024], BF16)
    th_h = din("theta_lay", [128, H], F32)
    qb_h = din("qkvu_b_lay", [128, 32], F32)
    bv_h = din("v_b_row", [1, 1024], BF16)
    ob_h = din("out_b_lay", [128, 8], F32)
    bp_h = din("bp_lay", [128, 2], F32)
    sim_h = din("sim_t", [4, 256], F32)
    slope_h = din("slope_row", [1, 8], F32)
    temp_h = din("temp_row", [1, 1], F32)
    aux_h = din("aux_row", [1, 4], F32)

    out_h = nc.dram_tensor("out", [8, 128, TOWN], F32, kind="ExternalOutput").ap()
    dbg_h = nc.dram_tensor("dbg", [128, 8 * T], BF16, kind="ExternalOutput").ap() \
        if (limit is not None) else None
    stop_dump = []
    scratch_h = nc.dram_tensor("scratch", [1, 64], F32, kind="Internal").ap()
    b_scr = Buf("scratch")

    oneskd_i = nc.inline_tensor(np.full((128, 128), 1.0 / DM, dtype=BF), "oneskd_i")
    ones_i = nc.inline_tensor(np.ones((128, 128), dtype=BF), "ones_i")
    identb_i = nc.inline_tensor(np.eye(128, dtype=np.float32).astype(BF), "identb_i")
    identf_i = nc.inline_tensor(np.eye(128, dtype=np.float32), "identf_i")
    onesrow_i = nc.inline_tensor(np.ones((1, 128), dtype=BF), "onesrow_i")

    stat = {}

    def salloc(name, shape, dt):
        t = nc.alloc_sbuf_tensor(name, list(shape), dt)
        b = Buf(name)
        stat[name] = (t, b)
        return t, b

    oneskd, b_oneskd = salloc("oneskd", [128, 128], BF16)
    onesb, b_onesb = salloc("onesb", [128, 128], BF16)
    identb, b_identb = salloc("identb", [128, 128], BF16)
    identf, b_identf = salloc("identf", [128, 128], F32)
    onesrow, b_onesrow = salloc("onesrow", [1, 128], BF16)
    thlay, b_thlay = salloc("thlay", [128, H], F32)
    th2pi, b_th2pi = salloc("th2pi", [128, H], F32)
    thoffq, b_thoffq = salloc("thoffq", [128, 32], F32)
    koffq, b_koffq = salloc("koffq", [128, 32], F32)
    offqt, b_offqt = salloc("offqt", [128, 4], F32)
    qbl, b_qbl = salloc("qbl", [128, 32], F32)
    bvrow, b_bvrow = salloc("bvrow", [1, 1024], BF16)
    obl, b_obl = salloc("obl", [128, 8], F32)
    bpl, b_bpl = salloc("bpl", [128, 2], F32)
    simt, b_simt = salloc("simt", [4, 256], F32)
    simtb, b_simtb = salloc("simtb", [4, 256], BF16)
    simsq, b_simsq = salloc("simsq", [4, 256], F32)
    simrn, b_simrn = salloc("simrn", [4, 1], F32)
    simn, b_simn = salloc("simn", [128, 8], BF16)
    slopes, b_slopes = salloc("slopes", [1, 8], F32)
    temps, b_temps = salloc("temps", [1, 1], F32)
    auxs, b_auxs = salloc("auxs", [1, 4], F32)
    srow, b_srow = salloc("srow", [1, 24], F32)
    smalls, b_smalls = salloc("smalls", [128, 24], F32)
    maskh, b_maskh = salloc("maskh", [128, 128], F32)
    epsln, b_epsln = salloc("epsln", [128, 1], F32)
    epsrms, b_epsrms = salloc("epsrms", [128, 1], F32)
    epsl2, b_epsl2 = salloc("epsl2", [128, 1], F32)
    pihalf, b_pihalf = salloc("pihalf", [128, 1], F32)
    ibase, b_ibase = salloc("ibase", [128, 1024], F32)
    wu, b_wu = salloc("wu", [15, 1024], BF16)
    wd, b_wd = salloc("wd", [128, 120], BF16)
    hpb, b_hpb = salloc("hpb", [15, TOWN], BF16)
    ret15, b_ret15 = salloc("ret15", [15, TOWN], BF16)
    mxt, b_mxt = salloc("mxt", [128, 1], F32)
    dvt, b_dvt = salloc("dvt", [128, 4], F32)
    sgt, b_sgt = salloc("sgt", [128, 4], F32)
    bet, b_bet = salloc("bet", [128, 1], F32)
    ext, b_ext = salloc("ext", [128, 4], F32)
    sut, b_sut = salloc("sut", [128, 1], F32)
    lwt, b_lwt = salloc("lwt", [128, 4], F32)
    rtt, b_rtt = salloc("rtt", [128, 15], F32)

    psum = []
    for i in range(8):
        t = nc.alloc_psum_tensor("pbank%d" % i, [128, 512], F32)
        psum.append((t, Buf("pbank%d" % i, exclusive=True)))
    psum_bf = {i: (psum[i][0].bitcast(BF16), psum[i][1]) for i in (6, 7)}

    arena = Arena(nc, em, (nc.sbuf_bytes_remaining // 4) - 32)

    def dma(dst_ap, src_ap, reads, writes, force=False):
        wl, rl = list(writes), list(reads)
        sem = "d_" + (wl[0].name if wl else rl[0].name)
        return em.op(SY, lambda e: e.dma_start(out=dst_ap, in_=src_ap),
                     reads=rl, writes=wl, inc=16, force=force, sem=sem)

    dma(oneskd[:], oneskd_i.ap()[:, :], [], [b_oneskd])
    dma(onesb[:], ones_i.ap()[:, :], [], [b_onesb])
    dma(identb[:], identb_i.ap()[:, :], [], [b_identb])
    dma(identf[:], identf_i.ap()[:, :], [], [b_identf])
    dma(onesrow[:], onesrow_i.ap()[:, :], [], [b_onesrow])
    dma(thlay[:], th_h[:, :], [], [b_thlay])
    dma(qbl[:], qb_h[:, :], [], [b_qbl])
    dma(bvrow[:], bv_h[:, :], [], [b_bvrow])
    dma(obl[:], ob_h[:, :], [], [b_obl])
    dma(bpl[:], bp_h[:, :], [], [b_bpl])
    dma(simt[:], sim_h[:, :], [], [b_simt])
    dma(slopes[:], slope_h[:, :], [], [b_slopes])
    dma(temps[:], temp_h[:, :], [], [b_temps])
    dma(auxs[:], aux_h[:, :], [], [b_auxs])

    arena.push("L")
    xbf, b_xbf = arena.alloc("xbf", 128, 8 * T, BF16)
    bxb = [[Buf("xbf_c%d_g%d" % (c, g)) for g in range(8)] for c in range(8)]
    wh0, b_wh0 = arena.alloc("wh0", 128, 4096, BF16)
    klo, b_klo = arena.alloc("klo", 128, T, BF16)
    khi, b_khi = arena.alloc("khi", 128, T, BF16)
    qlo, b_qlo = arena.alloc("qlo", 128, TOWN, BF16)
    qhi, b_qhi = arena.alloc("qhi", 128, TOWN, BF16)

    arena.push("R")
    gated, b_gated = arena.alloc("gated", 128, 16 if limit in ("noalias", "snap", "mega") else 8 * TOWN,
                                 BF16, "R")
    rmsac, b_rmsac = arena.alloc("rmsac", 128, 16 if limit in ("noalias", "snap", "mega") else TOWN,
                                 F32, "R")

    for c in range(8):
        dma(xbf[:, c * T:(c + 1) * T], xbf_h[c, :, :], [], bxb[c])
    dma(wh0[:, :], wh_h[0, :, :], [], [b_wh0])

    em.op(V, lambda e: e.memset(epsln[:], 1e-5), writes=[b_epsln])
    em.op(V, lambda e: e.memset(epsrms[:], 1e-6), writes=[b_epsrms])
    em.op(V, lambda e: e.memset(epsl2[:], 1e-24), writes=[b_epsl2])
    em.op(V, lambda e: e.memset(pihalf[:], PI / 2), writes=[b_pihalf])
    em.op(V, lambda e: e.memset(rmsac[:, :], 0.0), writes=[b_rmsac])
    em.op(G_, lambda e: e.iota(ibase[:], pattern=[[1, 1024]], base=0,
                               channel_multiplier=0,
                               allow_small_or_imprecise_dtypes=True),
          writes=[b_ibase])

    def emit_smalls_chain():
        em.op(V, lambda e: e.memset(srow[:, :], 0.0), writes=[b_srow])
        em.op(S, lambda e: e.activation(out=srow[0:1, 0:8], in_=slopes[:, :], func=AF.Exp),
              reads=[b_srow, b_slopes], writes=[b_srow])
        em.op(V, lambda e: e.tensor_scalar(out=srow[0:1, 0:8], in0=srow[0:1, 0:8],
                                           scalar1=1.0, scalar2=None, op0=ALU.subtract),
              reads=[b_srow], writes=[b_srow])
        em.op(V, lambda e: e.tensor_scalar(out=srow[0:1, 8:16], in0=srow[0:1, 0:8],
                                           scalar1=auxs[0:1, 0:1], scalar2=1.0,
                                           op0=ALU.mult, op1=ALU.add),
              reads=[b_srow, b_auxs], writes=[b_srow])
        em.op(V, lambda e: e.tensor_scalar(out=srow[0:1, 18:19], in0=temps[:, :],
                                           scalar1=CLAMP_MAX, scalar2=None, op0=ALU.min),
              reads=[b_temps], writes=[b_srow])
        em.op(S, lambda e: e.activation(out=srow[0:1, 17:18], in_=srow[0:1, 18:19],
                                        func=AF.Exp),
              reads=[b_srow], writes=[b_srow])
        em.op(V, lambda e: e.tensor_scalar(out=srow[0:1, 16:17], in0=srow[0:1, 17:18],
                                           scalar1=-1.0, scalar2=None, op0=ALU.mult),
              reads=[b_srow], writes=[b_srow])
        em.op(V, lambda e: e.tensor_copy(out=srow[0:1, 19:22], in_=auxs[0:1, 0:3]),
              reads=[b_auxs], writes=[b_srow])
        dma(scratch_h[0:1, 0:24], srow[:, :], [b_srow], [b_scr])
        dma(smalls[:], scratch_h[0:1, 0:24].to_broadcast((128, 24)), [b_scr], [b_smalls])

        em.op(V, lambda e: e.tensor_scalar(out=th2pi[:], in0=thlay[:, :],
                                           scalar1=1.0 / (2 * PI), scalar2=None,
                                           op0=ALU.mult),
              reads=[b_thlay], writes=[b_th2pi])
        for q in range(4):
            scol = 20 if q < 2 else 21

            def mkoff(e, q=q, scol=scol):
                return e.tensor_scalar(out=offqt[:, q:q + 1], in0=smalls[:, scol:scol + 1],
                                       scalar1=float(1024 * q), scalar2=None, op0=ALU.add)
            em.op(V, mkoff, reads=[b_smalls], writes=[b_offqt])

            def mkth(e, q=q):
                return e.tensor_scalar(out=thoffq[:, q * 8:(q + 1) * 8], in0=thlay[:, :],
                                       scalar1=offqt[:, q:q + 1], scalar2=None,
                                       op0=ALU.mult)
            em.op(V, mkth, reads=[b_thlay, b_offqt], writes=[b_thoffq])

            def mkko(e, q=q):
                return e.tensor_scalar(out=koffq[:, q * 8:(q + 1) * 8], in0=th2pi[:, :],
                                       scalar1=offqt[:, q:q + 1], scalar2=None,
                                       op0=ALU.mult)
            em.op(V, mkko, reads=[b_th2pi, b_offqt], writes=[b_koffq])

        em.op(V, lambda e: e.tensor_tensor(out=simsq[:], in0=simt[:], in1=simt[:],
                                           op=ALU.mult),
              reads=[b_simt], writes=[b_simsq])
        em.op(V, lambda e: e.reduce_sum(out=simrn[:], in_=simsq[:], axis=AXL.X),
              reads=[b_simsq], writes=[b_simrn])
        em.op(S, lambda e: e.activation(out=simrn[:], in_=simrn[:], func=AF.Ln,
                                        bias=epsl2[0:4, 0:1]),
              reads=[b_epsl2, b_simrn], writes=[b_simrn])
        em.op(S, lambda e: e.activation(out=simrn[:], in_=simrn[:], func=AF.Exp,
                                        scale=-0.5),
              reads=[b_simrn], writes=[b_simrn])
        em.op(V, lambda e: e.tensor_scalar(out=simtb[:], in0=simt[:],
                                           scalar1=simrn[0:4, 0:1], scalar2=None,
                                           op0=ALU.mult),
              reads=[b_simt, b_simrn], writes=[b_simtb])

    def emit_simn_tr():
        for pc in range(2):
            def trsim(e, pc=pc):
                return e.transpose(psum_bf[6][0][0:128, 0:4],
                                   simtb[:, pc * 128:(pc + 1) * 128], identb[0:4, 0:4])
            em.op(TE, trsim, reads=[b_simtb, b_identb], writes=[psum_bf[6][1]])

            def evsim(e, pc=pc):
                return e.tensor_copy(out=simn[:, pc * 4:(pc + 1) * 4],
                                     in_=psum_bf[6][0][0:128, 0:4])
            em.op(V, evsim, reads=[psum_bf[6][1]], writes=[b_simn])


    arena.push("R")
    varsb, b_varsb = arena.alloc("varsb", 128, 512, F32, "R")
    var2sb, b_var2sb = arena.alloc("var2sb", 128, 512, F32, "R")
    rsb, b_rsb = arena.alloc("rsb", 128, 512, F32, "R")
    t1sb, b_t1sb = arena.alloc("t1sb", 128, 512, F32, "R")
    xsqs = [arena.alloc("xsq%d" % i, 128, 512, BF16, "R") for i in range(2)]

    for grp in range(8):
        g0 = grp * 512
        pm, b_pm = psum[grp % 2]
        ps2, b_ps2 = psum[2 + grp % 2]
        for c in range(8):
            def mm_m(e, c=c, g0=g0, pm=pm):
                return e.matmul(pm[:, 0:512], oneskd[:],
                                xbf[:, c * T + g0: c * T + g0 + 512],
                                start=(c == 0), stop=(c == 7))
            em.op(TE, mm_m, reads=[b_oneskd, bxb[c][grp]], writes=[b_pm])
        for c in range(8):
            xsq, b_xsq = xsqs[c % 2]

            def sq(e, c=c, g0=g0, xsq=xsq):
                return e.activation(out=xsq[:, 0:512], func=AF.Square,
                                    in_=xbf[:, c * T + g0: c * T + g0 + 512])
            em.op(S, sq, reads=[bxb[c][grp]], writes=[b_xsq])

            def mm_s2(e, c=c, xsq=xsq, ps2=ps2):
                return e.matmul(ps2[:, 0:512], oneskd[:], xsq[:, 0:512],
                                start=(c == 0), stop=(c == 7))
            em.op(TE, mm_s2, reads=[b_oneskd, b_xsq], writes=[b_ps2])
        em.op(S, lambda e, pm=pm, varsb=varsb: e.activation(
                  out=varsb[:, 0:512], in_=pm[:, 0:512], func=AF.Square),
              reads=[b_pm], writes=[b_varsb])
        em.op(V, lambda e, ps2=ps2, varsb=varsb, var2sb=var2sb: e.tensor_tensor(
                  out=var2sb[:, 0:512], in0=ps2[:, 0:512], in1=varsb[:, 0:512],
                  op=ALU.subtract),
              reads=[b_ps2, b_varsb], writes=[b_var2sb])
        em.op(S, lambda e, var2sb=var2sb: e.activation(
                  out=var2sb[:, 0:512], in_=var2sb[:, 0:512],
                  func=AF.Ln, bias=epsln[:, 0:1]),
              reads=[b_epsln, b_var2sb], writes=[b_var2sb])
        em.op(S, lambda e, rsb=rsb, var2sb=var2sb: e.activation(
                  out=rsb[:, 0:512], in_=var2sb[:, 0:512],
                  func=AF.Exp, scale=-0.5),
              reads=[b_var2sb], writes=[b_rsb])
        for c in range(8):
            def sub(e, c=c, g0=g0, pm=pm, t1sb=t1sb):
                return e.tensor_tensor(out=t1sb[:, 0:512],
                                       in0=xbf[:, c * T + g0: c * T + g0 + 512],
                                       in1=pm[:, 0:512], op=ALU.subtract)
            em.op(V, sub, reads=[bxb[c][grp], b_pm], writes=[b_t1sb])

            def mul(e, c=c, g0=g0, t1sb=t1sb, rsb=rsb):
                return e.tensor_tensor(out=xbf[:, c * T + g0: c * T + g0 + 512],
                                       in0=t1sb[:, 0:512], in1=rsb[:, 0:512],
                                       op=ALU.mult)
            em.op(V, mul, reads=[b_t1sb, b_rsb], writes=[bxb[c][grp]])
        if grp == 0:
            emit_smalls_chain()
    emit_simn_tr()
    arena.pop("R")

    em.barrier()

    def finish():
        em.finalize()
        es = contextlib.ExitStack()
        sems = {e: es.enter_context(nc.semaphore("sem_" + e)) for e in ENGS}
        for sm in sorted(em.dma_sems):
            sems[sm] = es.enter_context(nc.semaphore("sem_" + sm))
        block = es.enter_context(nc.Block())

        @block.sync
        def _(eng):
            em.replay("sync", eng, sems)

        @block.tensor
        def _(eng):
            em.replay("tensor", eng, sems)

        @block.vector
        def _(eng):
            em.replay("vector", eng, sems)

        @block.scalar
        def _(eng):
            em.replay("scalar", eng, sems)

        @block.gpsimd
        def _(eng):
            em.replay("gpsimd", eng, sems)

        es.close()
        return nc

    if limit == "ln1":
        dma(dbg_h[:, :], xbf[:, 0:8 * T], [b for row in bxb for b in row], [], force=True)
        em.final_wait(SY, ["d_xbf"])
        return finish()
    if limit == "smalls":
        smf = nc.dram_tensor("smf", [128, 64], F32, kind="ExternalOutput").ap()
        b_smf = Buf("smf")
        em.barrier()
        dma(smf[:, 0:24], smalls[:, 0:24], [b_smalls], [b_smf], force=True)
        dma(smf[:, 24:56], thoffq[:, 0:32], [b_thoffq], [b_smf], force=True)
        em.final_wait(SY, ["d_smf"])
        return finish()

    QO, KO, VO, UO = 0, 128, 256, 384

    for h in range(H):
        al_ap = smalls[:, h:h + 1]
        c1_ap = smalls[:, 8 + h:9 + h]

        def mkmask(e, h=h):
            return e.affine_select(maskh[:],
                                   smalls[:, h:h + 1].broadcast_to([128, 128]),
                                   pattern=[[1, 128]], compare_op=ALU.is_ge,
                                   fill=0.0, base=0, channel_multiplier=-1)
        em.op(G_, mkmask, reads=[b_smalls], writes=[b_maskh])

        arena.push("L")
        thbufs = [arena.alloc("thbuf%d_%d" % (i, h), 128, 1024, F32)
                  for i in range(2)]
        kbufs = [arena.alloc("kbuf%d_%d" % (i, h), 128, 1024, F32)
                 for i in range(2)]
        sinb, b_sinb = arena.alloc("sinb_%d" % h, 128, T, BF16)
        cosb, b_cosb = arena.alloc("cosb_%d" % h, 128, T, BF16)
        ksil, b_ksil = arena.alloc("ksil_%d" % h, 128, T, BF16)
        qsil, b_qsil = arena.alloc("qsil_%d" % h, 128, TOWN, BF16)

        for q in range(4):
            qs = q * 1024
            thbuf, b_thbuf = thbufs[q % 2]
            kbuf, b_kbuf = kbufs[q % 2]

            def opth(e, q=q, h=h, thbuf=thbuf):
                return e.tensor_scalar(out=thbuf[:, 0:1024], in0=ibase[:],
                                       scalar1=thlay[:, h:h + 1],
                                       scalar2=thoffq[:, q * 8 + h: q * 8 + h + 1],
                                       op0=ALU.mult, op1=ALU.add)
            em.op(V, opth, reads=[b_ibase, b_thlay, b_thoffq], writes=[b_thbuf])

            def opk1(e, q=q, h=h, kbuf=kbuf):
                return e.tensor_scalar(out=kbuf[:, 0:1024], in0=ibase[:],
                                       scalar1=th2pi[:, h:h + 1],
                                       scalar2=koffq[:, q * 8 + h: q * 8 + h + 1],
                                       op0=ALU.mult, op1=ALU.add)
            em.op(V, opk1, reads=[b_ibase, b_th2pi, b_koffq], writes=[b_kbuf])
            em.op(V, lambda e, kbuf=kbuf: e.tensor_scalar(out=kbuf[:, 0:1024],
                                               in0=kbuf[:, 0:1024], scalar1=MAGIC,
                                               scalar2=MAGIC, op0=ALU.add,
                                               op1=ALU.subtract),
                  reads=[b_kbuf], writes=[b_kbuf])
            em.op(V, lambda e, kbuf=kbuf, thbuf=thbuf: e.scalar_tensor_tensor(out=kbuf[:, 0:1024],
                                                      in0=kbuf[:, 0:1024],
                                                      scalar=-2 * PI,
                                                      in1=thbuf[:, 0:1024],
                                                      op0=ALU.mult, op1=ALU.add),
                  reads=[b_kbuf, b_thbuf], writes=[b_kbuf])
            em.op(V, lambda e, kbuf=kbuf: e.tensor_scalar(out=kbuf[:, 0:1024],
                                               in0=kbuf[:, 0:1024],
                                               scalar1=float(np.float32(PI)),
                                               scalar2=-float(np.float32(PI)),
                                               op0=ALU.min, op1=ALU.max),
                  reads=[b_kbuf], writes=[b_kbuf])

            def opsin(e, qs=qs, kbuf=kbuf):
                return e.activation(out=sinb[:, qs:qs + 1024], in_=kbuf[:, 0:1024],
                                    func=AF.Sin)
            em.op(S, opsin, reads=[b_kbuf], writes=[b_sinb])
            em.op(S, lambda e, kbuf=kbuf, thbuf=thbuf: e.activation(out=thbuf[:, 0:1024], in_=kbuf[:, 0:1024],
                                            func=AF.Abs),
                  reads=[b_kbuf], writes=[b_thbuf])

            def opcos(e, qs=qs, thbuf=thbuf):
                return e.activation(out=cosb[:, qs:qs + 1024], in_=thbuf[:, 0:1024],
                                    func=AF.Sin, scale=-1.0, bias=pihalf[:, 0:1])
            em.op(S, opcos, reads=[b_thbuf, b_pihalf], writes=[b_cosb])

        for nt in range(8):
            pp, b_pp = psum[4 + nt % 2]
            for c in range(8):
                def mmk(e, c=c, nt=nt, pp=pp):
                    return e.matmul(pp[:, 0:512],
                                    wh0[:, c * 512 + KO: c * 512 + KO + 128],
                                    xbf[:, c * T + nt * 512: c * T + (nt + 1) * 512],
                                    start=(c == 0), stop=(c == 7))
                em.op(TE, mmk, reads=[b_wh0, bxb[c][nt]], writes=[b_pp])

            def evk(e, nt=nt, pp=pp, h=h):
                return e.activation(out=ksil[:, nt * 512:(nt + 1) * 512],
                                    in_=pp[:, 0:512], func=AF.Silu,
                                    bias=qbl[:, 8 + h: 9 + h])
            em.op(S, evk, reads=[b_pp, b_qbl], writes=[b_ksil])
        em.op(V, lambda e: e.tensor_tensor(out=klo[:, :], in0=ksil[:, :],
                                           in1=cosb[:, :], op=ALU.mult),
              reads=[b_ksil, b_cosb], writes=[b_klo])
        em.op(V, lambda e: e.tensor_tensor(out=khi[:, :], in0=ksil[:, :],
                                           in1=sinb[:, :], op=ALU.mult),
              reads=[b_ksil, b_sinb], writes=[b_khi])

        for nt in range(4):
            pp, b_pp = psum[4 + nt % 2]
            for c in range(8):
                def mmq(e, c=c, nt=nt, pp=pp):
                    return e.matmul(pp[:, 0:512],
                                    wh0[:, c * 512 + QO: c * 512 + QO + 128],
                                    xbf[:, c * T + nt * 512: c * T + (nt + 1) * 512],
                                    start=(c == 0), stop=(c == 7))
                em.op(TE, mmq, reads=[b_wh0, bxb[c][nt]], writes=[b_pp])

            def evq(e, nt=nt, pp=pp, h=h):
                return e.activation(out=qsil[:, nt * 512:(nt + 1) * 512],
                                    in_=pp[:, 0:512], func=AF.Silu,
                                    bias=qbl[:, h: h + 1])
            em.op(S, evq, reads=[b_pp, b_qbl], writes=[b_qsil])
        em.op(V, lambda e: e.tensor_tensor(out=qlo[:, :], in0=qsil[:, :],
                                           in1=cosb[:, 0:TOWN], op=ALU.mult),
              reads=[b_qsil, b_cosb], writes=[b_qlo])
        em.op(V, lambda e: e.tensor_tensor(out=qhi[:, :], in0=qsil[:, :],
                                           in1=sinb[:, 0:TOWN], op=ALU.mult),
              reads=[b_qsil, b_sinb], writes=[b_qhi])

        if limit == "h0prod" and h == 0:
            stop_dump.append((klo, b_klo, T))
            arena.pop("L")
            break

        if limit not in ("noalias", "mega"):
            arena.pop("L")

        arena.push("L")
        if limit == "noalias":
            pass
        klot, b_klot = arena.alloc("klot_%d" % h, 128, T, BF16)
        khit, b_khit = arena.alloc("khit_%d" % h, 128, T, BF16)
        vtm, b_vtm = arena.alloc("vtm_%d" % h, 128, T, BF16)
        ub, b_ub = arena.alloc("ub_%d" % h, 128, TOWN, BF16)
        Msb, b_Msb = arena.alloc("Msb_%d" % h, 128, NG * 256, F32)
        Mbf, b_Mbf = arena.alloc("Mbf_%d" % h, 128, NG * 256, BF16)
        etsb, b_etsb = arena.alloc("etsb_%d" % h, 128, 384, BF16)
        asq, b_asq = arena.alloc("asq_%d" % h, 128, 512, BF16)

        for nt in range(4):
            pp, b_pp = psum[4 + nt % 2]
            for c in range(8):
                def mmu(e, c=c, nt=nt, pp=pp):
                    return e.matmul(pp[:, 0:512],
                                    wh0[:, c * 512 + UO: c * 512 + UO + 128],
                                    xbf[:, c * T + nt * 512: c * T + (nt + 1) * 512],
                                    start=(c == 0), stop=(c == 7))
                em.op(TE, mmu, reads=[b_wh0, bxb[c][nt]], writes=[b_pp])

            def evu(e, nt=nt, pp=pp, h=h):
                return e.activation(out=ub[:, nt * 512:(nt + 1) * 512],
                                    in_=pp[:, 0:512], func=AF.Identity,
                                    bias=qbl[:, 24 + h: 25 + h])
            em.op(S, evu, reads=[b_pp, b_qbl], writes=[b_ub])

        for st in range(NCH):
            pp, b_pp = psum[st % 2]
            for c in range(8):
                def mmv(e, c=c, st=st, pp=pp):
                    return e.matmul(pp[:, 0:128],
                                    xbf[:, c * T + st * 128: c * T + (st + 1) * 128],
                                    wh0[:, c * 512 + VO: c * 512 + VO + 128],
                                    start=(c == 0),
                                    stop=(limit == "nok1" and c == 7))
                em.op(TE, mmv, reads=[b_wh0, bxb[c][st // 4]], writes=[b_pp])

            if limit == "nok1":
                pass
            else:
                def mmvb(e, pp=pp, h=h):
                    return e.matmul(pp[:, 0:128], onesrow[:, 0:128],
                                    bvrow[0:1, h * 128:(h + 1) * 128],
                                    start=False, stop=True)
                em.op(TE, mmvb, reads=[b_onesrow, b_bvrow], writes=[b_pp])

            def evv(e, st=st, pp=pp):
                return e.activation(out=vtm[:, st * 128:(st + 1) * 128],
                                    in_=pp[:, 0:128], func=AF.Copy)
            em.op(S, evv, reads=[b_pp], writes=[b_vtm])

        if limit == "h0v" and h == 0:
            stop_dump.append((vtm, b_vtm, T))
            arena.pop("L")
            break

        for src, b_src, dst, b_dst in ((klo, b_klo, klot, b_klot),
                                       (khi, b_khi, khit, b_khit)):
            for st in range(NCH):
                bk, tj = (st // 4) % 2, st % 4
                pbt, b_pbt = psum[bk]

                def tr(e, st=st, tj=tj, src=src, pbt=pbt):
                    return e.matmul(pbt[:, tj * 128:(tj + 1) * 128],
                                    src[:, st * 128:(st + 1) * 128], identb[:],
                                    start=True, stop=True)
                em.op(TE, tr, reads=[b_src, b_identb], writes=[b_pbt])
                if st % 4 == 3:
                    grp = st // 4
                    pbt2, b_pbt2 = psum[grp % 2]

                    def ev(e, grp=grp, dst=dst, pbt2=pbt2):
                        return e.tensor_copy(out=dst[:, grp * 512:(grp + 1) * 512],
                                             in_=pbt2[:, 0:512])
                    em.op(V, ev, reads=[b_pbt2], writes=[b_dst])

        if limit == "h0tp" and h == 0:
            stop_dump.append((klot, b_klot, T))
            arena.pop("L")
            break
        ksnap = b_ksnap = None
        if limit == "snap" and h == 0:
            ksnap, b_ksnap = arena.alloc("ksnap", 128, T, BF16)
            for ss in range(4):
                def snp(e, ss=ss, ksnap=ksnap):
                    return e.tensor_copy(out=ksnap[:, ss * 1024:(ss + 1) * 1024],
                                         in_=klot[:, ss * 1024:(ss + 1) * 1024])
                em.op(V, snp, reads=[b_klot], writes=[b_ksnap])

        if limit == "h0sx5" and h == 0:
            em.barrier()
        _stlist = list(range(NCH))
        _ehlist = ((0, klot, b_klot), (1, khit, b_khit))
        if limit == "h0sx1" and h == 0:
            _ehlist = ((0, vtm, b_vtm), (1, vtm, b_vtm))
        if limit == "h0sx2" and h == 0:
            _ehlist = ((0, klot, b_klot),)
        if limit == "h0sx3" and h == 0:
            _stlist = [0, 1, 2, 3, 16, 17, 18, 19]
        for st in _stlist:
            own = st < 16
            first = (st % 16) == 0
            last = (st % 16) == 15 or (limit == "h0sx3" and h == 0 and st % 16 == 3)
            for eh, kt, b_kt in _ehlist:
                pb, b_pb = psum[(0 if own else 2) + eh]

                def mms(e, st=st, kt=kt, pb=pb, first=first, last=last):
                    return e.matmul(pb[:, 0:128], kt[:, st * 128:(st + 1) * 128],
                                    vtm[:, st * 128:(st + 1) * 128],
                                    start=first, stop=last)
                em.op(TE, mms, reads=[b_kt, b_vtm], writes=[b_pb])

        if limit == "h0sx5" and h == 0:
            stop_dump.append((klot, b_klot, T))
            break
        if limit == "h0sx6" and h == 0:
            pdmp, b_pdmp = arena.alloc("pdmp", 128, 256, BF16)
            def evp6(e, pdmp=pdmp):
                return e.tensor_copy(out=pdmp[:, 0:128], in_=psum_bf[6][0][:, 0:128])
            em.op(V, evp6, reads=[psum_bf[6][1]], writes=[b_pdmp])
            def evp7(e, pdmp=pdmp):
                return e.tensor_copy(out=pdmp[:, 128:256], in_=psum_bf[7][0][:, 0:128])
            em.op(V, evp7, reads=[psum_bf[7][1]], writes=[b_pdmp])
            stop_dump.append((pdmp, b_pdmp, 256))
            break
        if limit == "noalias" and h == 0:
            stop_dump.append((klot, b_klot, T))
            break
        if limit == "snap" and h == 0:
            snap2, b_snap2 = arena.alloc("snap2", 128, 2 * T, BF16)
            for ss in range(4):
                def sn2(e, ss=ss, snap2=snap2):
                    return e.tensor_copy(out=snap2[:, ss * 1024:(ss + 1) * 1024],
                                         in_=ksnap[:, ss * 1024:(ss + 1) * 1024])
                em.op(V, sn2, reads=[b_ksnap], writes=[b_snap2])
                def sn3(e, ss=ss, snap2=snap2):
                    return e.tensor_copy(out=snap2[:, T + ss * 1024: T + (ss + 1) * 1024],
                                         in_=klot[:, ss * 1024:(ss + 1) * 1024])
                em.op(V, sn3, reads=[b_klot], writes=[b_snap2])
            stop_dump.append((snap2, b_snap2, 2 * T))
            break
        if limit in ("h0sx1", "h0sx2", "h0sx3") and h == 0:
            stop_dump.append((klot, b_klot, T))
            break
        if limit == "nok1" and h == 0:
            stop_dump.append((klot, b_klot, T))
            break
        if limit == "mega" and h == 0:
            md, b_md = arena.alloc("mega", 128, T, BF16, "R")
            for q4 in range(4):
                def c1(e, q4=q4, md=md):
                    return e.tensor_copy(out=md[:, q4 * 1024:(q4 + 1) * 1024],
                                         in_=khi[:, q4 * 1024:(q4 + 1) * 1024])
                em.op(V, c1, reads=[b_khi], writes=[b_md])
            stop_dump.append((md, b_md, T))
            break
        if limit in ("h0st2", "h0st3", "h0st4") and h == 0:
            stop_dump.append((klot if limit == "h0st2" else (vtm if limit == "h0st3" else khit),
                              b_klot if limit == "h0st2" else (b_vtm if limit == "h0st3" else b_khit), T))
            break
        if limit == "h0st" and h == 0:
            stdmp, b_stdmp = arena.alloc("stdmp", 128, 512, BF16)
            for i in range(4):
                def evst(e, i=i, stdmp=stdmp):
                    return e.tensor_copy(out=stdmp[:, i * 128:(i + 1) * 128],
                                         in_=psum[i][0][:, 0:128])
                em.op(V, evst, reads=[psum[i][1]], writes=[b_stdmp])
            stop_dump.append((stdmp, b_stdmp, 512))
            break

        for eh in range(2):
            pS, b_pS = psum[2 + eh]
            pO, b_pO = psum[0 + eh]

            def mi1(e, eh=eh, pS=pS, c1_ap=c1_ap):
                return e.tensor_scalar(out=Msb[:, eh * 128:(eh + 1) * 128],
                                       in0=pS[:, 0:128], scalar1=c1_ap,
                                       scalar2=None, op0=ALU.mult)
            em.op(V, mi1, reads=[b_pS, b_smalls], writes=[b_Msb])

            def mi2(e, eh=eh, pO=pO):
                return e.tensor_tensor(out=Msb[:, eh * 128:(eh + 1) * 128],
                                       in0=Msb[:, eh * 128:(eh + 1) * 128],
                                       in1=pO[:, 0:128], op=ALU.add)
            em.op(V, mi2, reads=[b_pO, b_Msb], writes=[b_Msb])
        em.op(V, lambda e: e.tensor_copy(out=Mbf[:, 0:256], in_=Msb[:, 0:256]),
              reads=[b_Msb], writes=[b_Mbf])

        for g in range(NG):
            pk, b_pk = psum[2 + g % 2]
            for eh, kt, b_kt in ((0, klot, b_klot), (1, khit, b_khit)):
                for j in range(2):
                    st = 2 * g + j

                    def mmkv(e, st=st, eh=eh, kt=kt, pk=pk, j=j):
                        return e.matmul(pk[:, eh * 128:(eh + 1) * 128],
                                        kt[:, st * 128:(st + 1) * 128],
                                        vtm[:, st * 128:(st + 1) * 128],
                                        start=(j == 0), stop=(j == 1))
                    em.op(TE, mmkv, reads=[b_kt, b_vtm], writes=[b_pk])
            if g < NG - 1:
                def mup(e, g=g, pk=pk, al_ap=al_ap):
                    return e.scalar_tensor_tensor(
                        out=Msb[:, (g + 1) * 256:(g + 2) * 256],
                        in0=pk[:, 0:256], scalar=al_ap,
                        in1=Msb[:, g * 256:(g + 1) * 256],
                        op0=ALU.mult, op1=ALU.add)
                em.op(V, mup, reads=[b_pk, b_Msb, b_smalls], writes=[b_Msb])

                def mcast(e, g=g):
                    return e.tensor_copy(out=Mbf[:, (g + 1) * 256:(g + 2) * 256],
                                         in_=Msb[:, (g + 1) * 256:(g + 2) * 256])
                em.op(V, mcast, reads=[b_Msb], writes=[b_Mbf])

        if limit == "h0m" and h == 0:
            stop_dump.append((Mbf, b_Mbf, NG * 256))
            arena.pop("L")
            break

        g0stage = 99
        if limit and limit.startswith("h0s") and h == 0 and limit[3:].isdigit():
            g0stage = int(limit[3:])
        ngrun = 1 if (limit == "h0g1" or (limit or "").startswith("h0s")) and h == 0 else NG
        for g in range(ngrun):
            qc0, qc1 = 2 * g, 2 * g + 1
            pe, b_pe = psum[4 + g % 2]
            po, b_po = psum[6 + g % 2]
            for eh, kf, b_kf, qf, b_qf in ((0, klo, b_klo, qlo, b_qlo),
                                           (1, khi, b_khi, qhi, b_qhi)):
                def mme0(e, kf=kf, qf=qf, pe=pe, g=g, qc0=qc0, eh=eh):
                    return e.matmul(pe[:, 0:256],
                                    kf[:, qc0 * 128:(qc0 + 1) * 128],
                                    qf[:, g * 256:(g + 1) * 256],
                                    start=(eh == 0), stop=(eh == 1))
                em.op(TE, mme0, reads=[b_kf, b_qf], writes=[b_pe])
            for eh, kf, b_kf, qf, b_qf in ((0, klo, b_klo, qlo, b_qlo),
                                           (1, khi, b_khi, qhi, b_qhi)):
                def mme1(e, kf=kf, qf=qf, pe=pe, qc1=qc1, eh=eh):
                    return e.matmul(pe[:, 256:384],
                                    kf[:, qc1 * 128:(qc1 + 1) * 128],
                                    qf[:, qc1 * 128:(qc1 + 1) * 128],
                                    start=(eh == 0), stop=(eh == 1))
                em.op(TE, mme1, reads=[b_kf, b_qf], writes=[b_pe])
            if g0stage < 2:
                continue
            em.op(V, lambda e, pe=pe: e.tensor_tensor(out=etsb[:, 0:128],
                                                      in0=pe[:, 0:128], in1=maskh[:],
                                                      op=ALU.mult),
                  reads=[b_pe, b_maskh], writes=[b_etsb])
            em.op(S, lambda e, pe=pe, al_ap=al_ap: e.activation(
                out=etsb[:, 128:256], in_=pe[:, 128:256], func=AF.Identity,
                scale=al_ap),
                reads=[b_pe, b_smalls], writes=[b_etsb])
            em.op(V, lambda e, pe=pe: e.tensor_tensor(out=etsb[:, 256:384],
                                                      in0=pe[:, 256:384], in1=maskh[:],
                                                      op=ALU.mult),
                  reads=[b_pe, b_maskh], writes=[b_etsb])

            if g0stage < 3:
                continue

            def mmx0(e, po=po, g=g):
                return e.matmul(po[:, 0:256], Mbf[:, g * 256: g * 256 + 128],
                                qlo[:, g * 256:(g + 1) * 256], start=True, stop=False)
            em.op(TE, mmx0, reads=[b_Mbf, b_qlo], writes=[b_po])

            def mmx1(e, po=po, g=g):
                return e.matmul(po[:, 0:256], Mbf[:, g * 256 + 128:(g + 1) * 256],
                                qhi[:, g * 256:(g + 1) * 256], start=False, stop=False)
            em.op(TE, mmx1, reads=[b_Mbf, b_qhi], writes=[b_po])

            if g0stage < 4:
                def mmx1b(e, po=po, g=g):
                    return e.matmul(po[:, 0:256], Mbf[:, g * 256 + 128:(g + 1) * 256],
                                    qhi[:, g * 256:(g + 1) * 256], start=False,
                                    stop=True)
                em.op(TE, mmx1b, reads=[b_Mbf, b_qhi], writes=[b_po])
                continue

            def mmi0(e, po=po, qc0=qc0):
                return e.matmul(po[:, 0:256], vtm[:, qc0 * 128:(qc0 + 1) * 128],
                                etsb[:, 0:256], start=False, stop=False)
            em.op(TE, mmi0, reads=[b_vtm, b_etsb], writes=[b_po])

            def mmi1(e, po=po, qc1=qc1):
                return e.matmul(po[:, 128:256], vtm[:, qc1 * 128:(qc1 + 1) * 128],
                                etsb[:, 256:384], start=False, stop=True)
            em.op(TE, mmi1, reads=[b_vtm, b_etsb], writes=[b_po])

            if g0stage < 5:
                continue

            def evg(e, po=po, g=g, h=h):
                return e.tensor_tensor(
                    out=gated[:, h * TOWN + g * 256: h * TOWN + (g + 1) * 256],
                    in0=po[:, 0:256], in1=ub[:, g * 256:(g + 1) * 256], op=ALU.mult)
            em.op(V, evg, reads=[b_po, b_ub], writes=[b_gated])

            if g0stage < 6:
                continue

            def evsq(e, po=po, g=g):
                return e.activation(out=asq[:, (g % 2) * 256:(g % 2) * 256 + 256],
                                    in_=po[:, 0:256], func=AF.Square)
            em.op(S, evsq, reads=[b_po], writes=[b_asq])

            if g0stage < 7:
                continue
            pr, b_pr = psum[0 + g % 2]

            def mmr(e, g=g, pr=pr):
                return e.matmul(pr[:, 0:256], oneskd[:],
                                asq[:, (g % 2) * 256:(g % 2) * 256 + 256],
                                start=True, stop=True)
            em.op(TE, mmr, reads=[b_oneskd, b_asq], writes=[b_pr])

            def accr(e, g=g, pr=pr):
                return e.tensor_tensor(out=rmsac[:, g * 256:(g + 1) * 256],
                                       in0=rmsac[:, g * 256:(g + 1) * 256],
                                       in1=pr[:, 0:256], op=ALU.add)
            em.op(V, accr, reads=[b_pr, b_rmsac], writes=[b_rmsac])

        if (limit in ("h0et", "h0g1")) and h == 0:
            stop_dump.append((gated, b_gated, 8 * TOWN))
            arena.pop("L")
            break

        arena.pop("L")

        if limit == "head0" and h == 0:
            break
        if h + 1 < H:
            dma(wh0[:, :], wh_h[h + 1, :, :], [], [b_wh0])

    if stop_dump:
        vv, bb, ww = stop_dump[0]
        em.barrier()
        dma(dbg_h[:, 0:ww], vv[:, 0:ww], [bb], [], force=True)
        em.final_wait(SY, ["d_" + bb.name])
        return finish()
    if limit == "head0":
        em.barrier()
        dma(dbg_h[:, 0:8 * TOWN], gated[:, 0:8 * TOWN], [b_gated], [], force=True)
        em.final_wait(SY, ["d_gated"])
        return finish()
    arena.pop("L")
    em.barrier()

    em.op(S, lambda e: e.activation(out=rmsac[:, :], in_=rmsac[:, :], func=AF.Ln,
                                    bias=epsrms[:, 0:1]),
          reads=[b_epsrms, b_rmsac], writes=[b_rmsac])
    em.op(S, lambda e: e.activation(out=rmsac[:, :], in_=rmsac[:, :], func=AF.Exp,
                                    scale=-0.5),
          reads=[b_rmsac], writes=[b_rmsac])

    arena.push("L")
    r1, b_r1 = arena.alloc("r1", 128, 8 * TOWN, F32)
    wf, b_wf = arena.alloc("wf", 128, 8192, BF16)
    wp, b_wp = arena.alloc("wp", 128, 2048, BF16)
    arena.push("L")
    wo, b_wo = arena.alloc("wo", 128, 8192, BF16)
    xf0, b_xf0 = arena.alloc("xf0", 128, TOWN, F32)
    xf1, b_xf1 = arena.alloc("xf1", 128, TOWN, F32)
    tmpo, b_tmpo = arena.alloc("tmpo", 128, 512, F32)

    dma(wo[:, :], wo_h[:, :], [], [b_wo])
    dma(wf[:, :], wf_h[:, :], [], [b_wf])
    dma(wp[:, :], wp_h[:, :], [], [b_wp])
    dma(wd[:], wd_h[:, :], [], [b_wd])
    dma(wu[:], wu_h[:, :], [], [b_wu])

    xfs = [(xf0, b_xf0), (xf1, b_xf1)]
    for oc in range(8):
        xf, b_xf = xfs[oc % 2]
        dma(xf[:, :], xf32_h[oc, :, :], [], [b_xf])
        for nt in range(4):
            pp, b_pp = psum[nt % 2]
            for c in range(8):
                def mmo(e, c=c, oc=oc, nt=nt, pp=pp):
                    return e.matmul(pp[:, 0:512],
                                    wo[:, c * 1024 + oc * 128: c * 1024 + (oc + 1) * 128],
                                    gated[:, c * TOWN + nt * 512: c * TOWN + (nt + 1) * 512],
                                    start=(c == 0), stop=(c == 7))
                em.op(TE, mmo, reads=[b_wo, b_gated], writes=[b_pp])

            def evo1(e, nt=nt, pp=pp):
                return e.tensor_tensor(out=tmpo[:, 0:512], in0=pp[:, 0:512],
                                       in1=rmsac[:, nt * 512:(nt + 1) * 512],
                                       op=ALU.mult)
            em.op(V, evo1, reads=[b_pp, b_rmsac], writes=[b_tmpo])

            def evo2(e, oc=oc, nt=nt, xf=xf):
                return e.scalar_tensor_tensor(
                    out=r1[:, oc * TOWN + nt * 512: oc * TOWN + (nt + 1) * 512],
                    in0=tmpo[:, 0:512], scalar=obl[:, oc:oc + 1],
                    in1=xf[:, nt * 512:(nt + 1) * 512], op0=ALU.add, op1=ALU.add)
            em.op(V, evo2, reads=[b_tmpo, b_obl, b_xf], writes=[b_r1])

    arena.pop("L")
    arena.pop("R")
    em.barrier()

    arena.push("R")
    xc2, b_xc2 = arena.alloc("xc2", 128, 8 * TOWN, BF16, "R")
    varsb, b_varsb = arena.alloc("varsb4", 128, 512, F32, "R")
    var2sb, b_var2sb = arena.alloc("var2sb4", 128, 512, F32, "R")
    rsb, b_rsb = arena.alloc("rsb4", 128, 512, F32, "R")
    t1sb, b_t1sb = arena.alloc("t1sb4", 128, 512, F32, "R")
    xsqs = [arena.alloc("xsq4_%d" % i, 128, 512, BF16, "R") for i in range(2)]
    rbfs = [arena.alloc("rbf4_%d" % i, 128, 512, BF16, "R") for i in range(2)]

    for grp in range(4):
        g0 = grp * 512
        pm, b_pm = psum[2 + grp % 2]
        ps2, b_ps2 = psum[4 + grp % 2]
        for c in range(8):
            rbf, b_rbf = rbfs[c % 2]

            def cst(e, c=c, g0=g0, rbf=rbf):
                return e.activation(out=rbf[:, 0:512], func=AF.Copy,
                                    in_=r1[:, c * TOWN + g0: c * TOWN + g0 + 512])
            em.op(S, cst, reads=[b_r1], writes=[b_rbf])

            def mm_m(e, c=c, rbf=rbf, pm=pm):
                return e.matmul(pm[:, 0:512], oneskd[:], rbf[:, 0:512],
                                start=(c == 0), stop=(c == 7))
            em.op(TE, mm_m, reads=[b_oneskd, b_rbf], writes=[b_pm])
        for c in range(8):
            xsq, b_xsq = xsqs[c % 2]

            def sq(e, c=c, g0=g0, xsq=xsq):
                return e.activation(out=xsq[:, 0:512], func=AF.Square,
                                    in_=r1[:, c * TOWN + g0: c * TOWN + g0 + 512])
            em.op(S, sq, reads=[b_r1], writes=[b_xsq])

            def mm_s2(e, c=c, xsq=xsq, ps2=ps2):
                return e.matmul(ps2[:, 0:512], oneskd[:], xsq[:, 0:512],
                                start=(c == 0), stop=(c == 7))
            em.op(TE, mm_s2, reads=[b_oneskd, b_xsq], writes=[b_ps2])
        em.op(S, lambda e, pm=pm: e.activation(out=varsb[:, 0:512],
                                               in_=pm[:, 0:512], func=AF.Square),
              reads=[b_pm], writes=[b_varsb])
        em.op(V, lambda e, ps2=ps2: e.tensor_tensor(out=var2sb[:, 0:512],
                                                    in0=ps2[:, 0:512],
                                                    in1=varsb[:, 0:512],
                                                    op=ALU.subtract),
              reads=[b_ps2, b_varsb], writes=[b_var2sb])
        em.op(S, lambda e: e.activation(out=var2sb[:, 0:512], in_=var2sb[:, 0:512],
                                        func=AF.Ln, bias=epsln[:, 0:1]),
              reads=[b_epsln, b_var2sb], writes=[b_var2sb])
        em.op(S, lambda e: e.activation(out=rsb[:, 0:512], in_=var2sb[:, 0:512],
                                        func=AF.Exp, scale=-0.5),
              reads=[b_var2sb], writes=[b_rsb])
        for c in range(8):
            def sub(e, c=c, g0=g0, pm=pm):
                return e.tensor_tensor(out=t1sb[:, 0:512],
                                       in0=r1[:, c * TOWN + g0: c * TOWN + g0 + 512],
                                       in1=pm[:, 0:512], op=ALU.subtract)
            em.op(V, sub, reads=[b_r1, b_pm], writes=[b_t1sb])

            def mul(e, c=c, g0=g0):
                return e.tensor_tensor(out=xc2[:, c * TOWN + g0: c * TOWN + g0 + 512],
                                       in0=t1sb[:, 0:512], in1=rsb[:, 0:512],
                                       op=ALU.mult)
            em.op(V, mul, reads=[b_t1sb, b_rsb], writes=[b_xc2])

    zsb, b_zsb = arena.alloc("zsb", 128, 2 * TOWN, F32, "R")
    zsq4 = [arena.alloc("zsq4_%d" % i, 128, 512, BF16, "R") for i in range(2)]
    rn4 = [arena.alloc("rn4_%d" % i, 128, 512, F32, "R") for i in range(2)]
    pjb, b_pjb = arena.alloc("pjb", 128, 2 * TOWN, BF16, "R")

    for pc in range(2):
        for nt in range(4):
            pp, b_pp = psum[nt % 2]
            for c in range(8):
                def mmp(e, c=c, pc=pc, nt=nt, pp=pp):
                    return e.matmul(pp[:, 0:512],
                                    wp[:, c * 256 + pc * 128: c * 256 + (pc + 1) * 128],
                                    xc2[:, c * TOWN + nt * 512: c * TOWN + (nt + 1) * 512],
                                    start=(c == 0), stop=(c == 7))
                em.op(TE, mmp, reads=[b_wp, b_xc2], writes=[b_pp])

            def evp(e, pc=pc, nt=nt, pp=pp):
                return e.activation(
                    out=zsb[:, pc * TOWN + nt * 512: pc * TOWN + (nt + 1) * 512],
                    in_=pp[:, 0:512], func=AF.Identity, bias=bpl[:, pc:pc + 1])
            em.op(S, evp, reads=[b_pp, b_bpl], writes=[b_zsb])
    for nt in range(4):
        pp, b_pp = psum[2 + nt % 2]
        for pc in range(2):
            zq, b_zq = zsq4[pc]

            def sqz(e, pc=pc, nt=nt, zq=zq):
                return e.activation(out=zq[:, 0:512], func=AF.Square,
                                    in_=zsb[:, pc * TOWN + nt * 512: pc * TOWN + (nt + 1) * 512])
            em.op(S, sqz, reads=[b_zsb], writes=[b_zq])

            def mmn(e, pc=pc, zq=zq, pp=pp):
                return e.matmul(pp[:, 0:512], onesb[:], zq[:, 0:512],
                                start=(pc == 0), stop=(pc == 1))
            em.op(TE, mmn, reads=[b_onesb, b_zq], writes=[b_pp])
        rn, b_rn = rn4[nt % 2]

        def evn(e, pp=pp, rn=rn):
            return e.activation(out=rn[:, 0:512], in_=pp[:, 0:512], func=AF.Ln,
                                bias=epsl2[:, 0:1])
        em.op(S, evn, reads=[b_pp, b_epsl2], writes=[b_rn])
        em.op(S, lambda e, rn=rn: e.activation(out=rn[:, 0:512], in_=rn[:, 0:512],
                                               func=AF.Exp, scale=-0.5),
              reads=[b_rn], writes=[b_rn])
        for pc in range(2):
            eng = V

            def pj(e, pc=pc, nt=nt, rn=rn):
                return e.tensor_tensor(
                    out=pjb[:, pc * TOWN + nt * 512: pc * TOWN + (nt + 1) * 512],
                    in0=zsb[:, pc * TOWN + nt * 512: pc * TOWN + (nt + 1) * 512],
                    in1=rn[:, 0:512], op=ALU.mult)
            em.op(eng, pj, reads=[b_zsb, b_rn], writes=[b_pjb])

    for st in range(16):
        pp, b_pp = psum[4 + st % 2]
        for pc in range(2):
            def mml(e, pc=pc, st=st, pp=pp):
                return e.matmul(pp[:, 0:4],
                                pjb[:, pc * TOWN + st * 128: pc * TOWN + (st + 1) * 128],
                                simn[:, pc * 4:(pc + 1) * 4],
                                start=(pc == 0), stop=(pc == 1))
            em.op(TE, mml, reads=[b_pjb, b_simn], writes=[b_pp])
        em.op(V, lambda e, pp=pp: e.reduce_max(out=mxt[:], in_=pp[:, 0:4], axis=AXL.X),
              reads=[b_pp], writes=[b_mxt])
        em.op(V, lambda e, pp=pp: e.tensor_tensor(
            out=dvt[:], in0=pp[:, 0:4],
            in1=mxt[:, 0:1].broadcast_to([128, 4]), op=ALU.subtract),
            reads=[b_pp, b_mxt], writes=[b_dvt])
        em.op(S, lambda e: e.sign(out=sgt[:], in_=dvt[:]),
              reads=[b_dvt], writes=[b_sgt])
        em.op(V, lambda e: e.tensor_scalar(out=bet[:], in0=mxt[:],
                                           scalar1=smalls[:, 16:17], scalar2=None,
                                           op0=ALU.mult),
              reads=[b_mxt, b_smalls], writes=[b_bet])

        def evex(e, pp=pp):
            return e.activation(out=ext[:], in_=pp[:, 0:4], func=AF.Exp,
                                scale=smalls[:, 17:18], bias=bet[:, 0:1],
                                accum_out=sut[:])
        em.op(S, evex, reads=[b_pp, b_bet, b_smalls], writes=[b_ext, b_sut])
        em.op(V, lambda e: e.reciprocal(out=sut[:], in_=sut[:]),
              reads=[b_sut], writes=[b_sut])
        em.op(V, lambda e: e.tensor_scalar(out=lwt[:], in0=ext[:],
                                           scalar1=sut[:, 0:1], scalar2=None,
                                           op0=ALU.mult),
              reads=[b_ext, b_sut], writes=[b_lwt])
        em.op(V, lambda e: e.scalar_tensor_tensor(out=lwt[:], in0=sgt[:], scalar=1.0,
                                                  in1=lwt[:], op0=ALU.add,
                                                  op1=ALU.mult),
              reads=[b_sgt, b_lwt], writes=[b_lwt])
        em.op(V, lambda e: e.tensor_copy(out=rtt[:, 0:1], in_=lwt[:, 0:1]),
              reads=[b_lwt], writes=[b_rtt])
        em.op(V, lambda e: e.tensor_copy(out=rtt[:, 1:3],
                                         in_=lwt[:, 1:2].broadcast_to([128, 2])),
              reads=[b_lwt], writes=[b_rtt])
        em.op(V, lambda e: e.tensor_copy(out=rtt[:, 3:7],
                                         in_=lwt[:, 2:3].broadcast_to([128, 4])),
              reads=[b_lwt], writes=[b_rtt])
        em.op(V, lambda e: e.tensor_copy(out=rtt[:, 7:15],
                                         in_=lwt[:, 3:4].broadcast_to([128, 8])),
              reads=[b_lwt], writes=[b_rtt])

        def trr(e):
            return e.transpose(psum[6][0][0:15, 0:128], rtt[:, 0:15], identf[:])
        em.op(TE, trr, reads=[b_rtt, b_identf], writes=[psum[6][1]])

        def evr(e, st=st):
            return e.tensor_copy(out=ret15[0:15, st * 128:(st + 1) * 128],
                                 in_=psum[6][0][0:15, 0:128])
        em.op(V, evr, reads=[psum[6][1]], writes=[b_ret15])

    for nt in range(4):
        pp, b_pp = psum[2 + nt % 2]
        for c in range(8):
            def mmd(e, c=c, nt=nt, pp=pp):
                return e.matmul(pp[0:15, 0:512], wd[:, c * 15:(c + 1) * 15],
                                xc2[:, c * TOWN + nt * 512: c * TOWN + (nt + 1) * 512],
                                start=(c == 0), stop=(c == 7))
            em.op(TE, mmd, reads=[b_wd, b_xc2], writes=[b_pp])

        def evh(e, nt=nt, pp=pp):
            return e.tensor_tensor(out=hpb[0:15, nt * 512:(nt + 1) * 512],
                                   in0=pp[0:15, 0:512],
                                   in1=ret15[0:15, nt * 512:(nt + 1) * 512],
                                   op=ALU.mult)
        em.op(V, evh, reads=[b_pp, b_ret15], writes=[b_hpb])

    ost0, b_ost0 = arena.alloc("ost0", 128, TOWN, F32, "R")
    ost1, b_ost1 = arena.alloc("ost1", 128, TOWN, F32, "R")
    osts = [(ost0, b_ost0), (ost1, b_ost1)]
    for oc in range(8):
        ost, b_ost = osts[oc % 2]
        for nt in range(4):
            pp, b_pp = psum[nt % 2]
            for c in range(8):
                def mmb(e, c=c, oc=oc, nt=nt, pp=pp):
                    return e.matmul(pp[:, 0:512],
                                    wf[:, c * 1024 + oc * 128: c * 1024 + (oc + 1) * 128],
                                    xc2[:, c * TOWN + nt * 512: c * TOWN + (nt + 1) * 512],
                                    start=(c == 0), stop=False)
                em.op(TE, mmb, reads=[b_wf, b_xc2], writes=[b_pp])

            def mmup(e, oc=oc, nt=nt, pp=pp):
                return e.matmul(pp[:, 0:512], wu[:, oc * 128:(oc + 1) * 128],
                                hpb[0:15, nt * 512:(nt + 1) * 512],
                                start=False, stop=True)
            em.op(TE, mmup, reads=[b_wu, b_hpb], writes=[b_pp])

            def evf(e, oc=oc, nt=nt, pp=pp, ost=ost):
                return e.tensor_tensor(
                    out=ost[:, nt * 512:(nt + 1) * 512], in0=pp[:, 0:512],
                    in1=r1[:, oc * TOWN + nt * 512: oc * TOWN + (nt + 1) * 512],
                    op=ALU.add)
            em.op(V, evf, reads=[b_pp, b_r1], writes=[b_ost])
        dma(out_h[oc, :, :], ost[:, :], [b_ost], [], force=True)

    arena.pop("R")
    arena.pop("L")

    em.final_wait(SY, ["d_ost0", "d_ost1"])
    return finish()


def make_inputs(ins):
    x = np.asarray(ins["L0_attn_x"], dtype=np.float32)
    qkvu_w = np.asarray(ins["L0_attn_qkvu_w"], dtype=np.float32)
    qkvu_b = np.asarray(ins["L0_attn_qkvu_b"], dtype=np.float32)
    out_w = np.asarray(ins["L0_attn_out_w"], dtype=np.float32)
    out_b = np.asarray(ins["L0_attn_out_b"], dtype=np.float32)
    theta = np.asarray(ins["L0_attn_theta"], dtype=np.float32)
    slope = np.asarray(ins["L0_attn_slope_rate"], dtype=np.float32)
    W = np.asarray(ins["L0_ffn_W"], dtype=np.float32)
    Wp = np.asarray(ins["L0_ffn_Wp"], dtype=np.float32)
    bp = np.asarray(ins["L0_ffn_bp"], dtype=np.float32)
    sim = np.asarray(ins["L0_ffn_sim"], dtype=np.float32)
    temp = np.asarray(ins["L0_ffn_temp"], dtype=np.float32)
    Wd = np.concatenate([np.asarray(ins["L0_ffn_Wd%d" % i], dtype=np.float32)
                         for i in range(4)], axis=0)
    Wu = np.concatenate([np.asarray(ins["L0_ffn_Wu%d" % i], dtype=np.float32)
                         for i in range(4)], axis=1)

    A = qkvu_w.reshape(4, 8, 128, 8, 128)
    w_qkvu = np.ascontiguousarray(
        A.transpose(1, 4, 3, 0, 2).reshape(8, 128, 4096)).astype(BF)
    w_out = np.ascontiguousarray(
        out_w.T.reshape(8, 128, 1024).transpose(1, 0, 2).reshape(128, 8192)).astype(BF)
    w_ffn = np.ascontiguousarray(
        W.T.reshape(8, 128, 1024).transpose(1, 0, 2).reshape(128, 8192)).astype(BF)
    w_proj = np.ascontiguousarray(
        Wp.T.reshape(8, 128, 256).transpose(1, 0, 2).reshape(128, 2048)).astype(BF)
    w_down = np.ascontiguousarray(
        Wd.T.reshape(8, 128, 15).transpose(1, 0, 2).reshape(128, 120)).astype(BF)
    w_up = np.ascontiguousarray(Wu.T).astype(BF)
    theta_lay = np.ascontiguousarray(theta.reshape(8, 128).T)
    qkvu_b_lay = np.ascontiguousarray(qkvu_b.reshape(32, 128).T)
    v_b_row = np.ascontiguousarray(qkvu_b[2048:3072].reshape(1, 1024)).astype(BF)
    out_b_lay = np.ascontiguousarray(out_b.reshape(8, 128).T)
    bp_lay = np.ascontiguousarray(bp.reshape(2, 128).T)
    sim_t = np.ascontiguousarray(sim.T)
    slope_row = np.ascontiguousarray(slope.reshape(1, 8))
    temp_row = np.ascontiguousarray(temp.reshape(1, 1))

    shared = dict(w_qkvu=w_qkvu, w_out=w_out, w_ffn=w_ffn, w_proj=w_proj,
                  w_down=w_down, w_up=w_up, theta_lay=theta_lay,
                  qkvu_b_lay=qkvu_b_lay, v_b_row=v_b_row, out_b_lay=out_b_lay,
                  bp_lay=bp_lay, sim_t=sim_t, slope_row=slope_row,
                  temp_row=temp_row)

    in_maps = []
    for core in range(NCORES):
        b = core // 2
        parity = core % 2
        off = TOWN * parity
        xt = x[b].T
        own = xt[:, off:off + TOWN]
        part = xt[:, TOWN:2 * TOWN] if parity == 0 else xt[:, 0:TOWN]
        xr = np.concatenate([own, part], axis=1)
        xbf = np.ascontiguousarray(xr.reshape(8, 128, T)).astype(BF)
        xf32 = np.ascontiguousarray(own.reshape(8, 128, TOWN)).astype(np.float32)
        aux_row = np.array([[float(parity), float(off), float(-off), 0.0]],
                           dtype=np.float32)
        m = dict(shared)
        m["xbf"] = xbf
        m["xf32"] = xf32
        m["aux_row"] = aux_row
        in_maps.append(m)
    return in_maps


_NC_CACHE = None


def kernel(**inputs):
    global _NC_CACHE
    in_maps = make_inputs(inputs)
    if _NC_CACHE is None:
        _NC_CACHE = build_program()
    res = run_bass_kernel_spmd(_NC_CACHE, in_maps, core_ids=list(range(NCORES)))
    out = np.empty((B, T, DM), dtype=np.float32)
    for core in range(NCORES):
        b = core // 2
        off = TOWN * (core % 2)
        o = res.results[core]["out"].reshape(DM, TOWN)
        out[b, off:off + TOWN, :] = o.T
    return out
